# version 08
# baseline (speedup 1.0000x reference)
import contextlib
import numpy as np
import ml_dtypes
import concourse.bass as bass
import concourse.bass_isa as bass_isa
import concourse.mybir as mybir
from concourse.bass_utils import run_bass_kernel_spmd

F32 = mybir.dt.float32
BF16 = mybir.dt.bfloat16
I32 = mybir.dt.int32
AF = mybir.ActivationFunctionType
ALU = mybir.AluOpType
AX = mybir.AxisListType

NCORES = 8
B, S, D, H, HD, I = 2, 2048, 1024, 16, 64, 3072
T = B * S
NT = T // 128
NDC = D // 128
NIC = I // 128
NTK = T // 512
QMAX = 32767.0
RSH = 32768.0
EPS = 1e-5
SCL_RS1, SCL_AU, SCL_AG, SCL_S3, SCL_AM, SCL_AO, SCL_S5 = 0, 1, 2, 3, 4, 5, 6

_SEMH = {}


class Eng:
    def __init__(self, key):
        self.key, self.n, self.ops = key, 0, []

    def emit(self, fn):
        self.n += 1
        tgt = self.n

        def wrapped(eng, _fn=fn, _E=self):
            _fn(eng).then_inc(_E.semh, 1)
        self.ops.append(wrapped)
        return (self.key, tgt)

    def dma(self, fn):
        self.n += 16
        tgt = self.n

        def wrapped(eng, _fn=fn, _E=self):
            _fn(eng).then_inc(_E.semh, 16)
        self.ops.append(wrapped)
        return (self.key, tgt)

    def wait(self, token):
        if token is None:
            return
        key, v = token
        self.ops.append(lambda eng, _k=key, _v=v: eng.wait_ge(_SEMH[_k], _v))

    def seq(self, fns):
        for i, fn in enumerate(fns):
            self.n += 1
            tgt = self.n

            def wrapped(eng, _fn=fn, _E=self):
                _fn(eng).then_inc(_E.semh, 1)
            self.ops.append(wrapped)
            if i + 1 < len(fns):
                self.ops.append(lambda eng, _k=self.key, _v=tgt: eng.wait_ge(_SEMH[_k], _v))
        return (self.key, self.n)


def build_program(dbg=(), cut=None):
    nc = bass.Bass("TRN2", target_bir_lowering=False, detect_race_conditions=False)
    xin = nc.dram_tensor("x", [B, S, D], F32, kind="ExternalInput")
    w_in = {"wu": nc.dram_tensor("wu", [I, D], F32, kind="ExternalInput"),
            "wg": nc.dram_tensor("wg", [I, D], F32, kind="ExternalInput")}
    wd_in = nc.dram_tensor("wd", [D, I], F32, kind="ExternalInput")
    out_d = nc.dram_tensor("out", [B, S, D], F32, kind="ExternalOutput")
    sp = {k: nc.dram_tensor("sp" + k, [NIC, NTK, 128, 512], BF16)
          for k in ["U", "G", "M1", "Rup", "Uu"]}
    sout_d = nc.dram_tensor("sout_d", [NT, 128, D], BF16)
    arin = [nc.dram_tensor(f"arin{k}", [1, 8], F32) for k in range(4)]
    arout = [nc.dram_tensor(f"arout{k}", [1, 8], F32) for k in range(4)]

    dbspec = {"xnT": ([128, NDC * T], BF16), "vbuf": ([128, NT * 1040], BF16),
              "bias": ([128, 32], F32), "scales": ([128, 8], F32),
              "rx": ([128, NDC * T], BF16), "wdt": ([128, NIC * 1024], BF16),
              "sm1": ([1, 60], F32)}
    dbg_out = {nm: nc.dram_tensor("dbg_" + nm, *dbspec[nm], kind="ExternalOutput")
               for nm in dbg}

    es = contextlib.ExitStack()
    sb = lambda n, shp, dt: es.enter_context(nc.sbuf_tensor(n, shp, dt))

    xnT = sb("xnT", [128, NDC * T], BF16)
    rxb = sb("rxb", [128, NDC * T], BF16)
    vbuf = sb("vbuf", [128, NT * 1040], BF16)
    scr = sb("scr", [128, 2304], F32)
    wt = sb("wt", [128, 2048], BF16)
    onesb = sb("onesb", [1, 128], BF16)
    smB = sb("smB", [1, 64], BF16)
    idnb = sb("idnb", [128, 128], BF16)
    inds = sb("inds", [128, 128], BF16)
    bias_all = sb("bias_all", [128, 32], F32)
    scl = sb("scl", [128, 8], F32)
    smf = sb("smf", [128, 20], F32)
    wsc = sb("wsc", [128, 16], F32)
    wss = sb("wss", [128, 16], F32)
    sm1 = sb("sm1", [1, 60], F32)
    g1acc = sb("g1acc", [128, 1], BF16)
    mx = {k: sb("mx" + k, [128, 1], BF16) for k in ["U", "G", "M1", "Uu"]}
    mbh = sb("mbh", [128, 2], BF16)
    arsb = sb("arsb", [1, 8], F32)

    scrB = scr.bitcast(BF16)
    scrI = scr.bitcast(I32)
    rxF = rxb.bitcast(F32)
    rxI = rxb.bitcast(I32)
    vbF = vbuf.bitcast(F32)

    PS = [es.enter_context(nc.psum_tensor(f"ps{k}", [128, 512], F32)) for k in range(8)]
    PSB = [p.bitcast(BF16) for p in PS]

    engs = {k: Eng(k) for k in ["q", "t", "v", "s", "g"]}
    for k, e in engs.items():
        e.semh = es.enter_context(nc.semaphore("sem_" + k))
        _SEMH[k] = e.semh
    Q, TE, V, SE, G = (engs[k] for k in ["q", "t", "v", "s", "g"])


    PH = {}

    def mark_phase(name):
        PH[name] = {k: (len(e.ops), e.n) for k, e in engs.items()}
    sumx, sumsq = smf[:, 0:1], smf[:, 1:2]
    mean, msq = smf[:, 2:3], smf[:, 3:4]
    varep, sd = smf[:, 4:5], smf[:, 5:6]
    r0n, hhn, f1n = smf[:, 6:7], smf[:, 7:8], smf[:, 8:9]
    rstd, negmr, tmp1 = smf[:, 9:10], smf[:, 10:11], smf[:, 11:12]
    rd4 = smf[:, 12:16]

    xnT3 = xnT[:, :].rearrange("p (c t) -> p c t", t=T)

    def g_init1(g):
        g.memset(onesb[:, :], 1.0)
        g.memset(idnb[:, :], 1.0)
        g.memset(inds[:, :], 0.0)
        g.memset(g1acc[:, :], 0.0)
        for k in mx:
            g.memset(mx[k][:, :], 0.0)
        g.memset(mbh[:, :], 0.0)
        g.memset(arsb[:, :], 0.0)
        return g.memset(vbuf[:, :].rearrange("p (a c) -> p a c", c=65)[:, :, 64:65], 1.0)
    t_i1 = G.emit(g_init1)
    G.wait(t_i1)
    t_i2 = G.emit(lambda g: g.affine_select(
        out=idnb[:, :], in_=idnb[:, :], pattern=[[1, 128]],
        compare_op=ALU.is_equal, fill=0.0, base=0, channel_multiplier=-1))
    G.wait(t_i2)

    def g_init3(g):
        for ch in range(NDC):
            g.memset(inds[0:64, 16 * ch + 2 * ch:16 * ch + 2 * ch + 1], 1.0)
            last = g.memset(inds[64:128, 16 * ch + 2 * ch + 1:16 * ch + 2 * ch + 2], 1.0)
        return last
    tok_consts = G.emit(g_init3)

    xst = scr[:, 0:1024]
    xn_tok = scrB[:, 2048:3072]
    xsq = scrB[:, 3072:4096]
    mark_phase("consts")
    tok_ln = tok_consts
    tok_ev = None
    tok_sqf = None
    tok_sqf_g = None
    tok_tnsq = None
    tok_nsqd = None
    for tt in range(NT):
        if tt in (1, 2, 8, 16, 24):
            mark_phase(f"p0t{tt}")
        b, t0 = tt // 16, (tt % 16) * 128
        col = b * S + t0
        Q.wait(tok_ln)
        tok_x = Q.dma(lambda q, _b=b, _t0=t0:
                      q.dma_start(out=xst, in_=xin[_b, _t0:_t0 + 128, :]))
        V.wait(tok_x)

        V.seq([
            lambda v: v.reduce_sum(out=sumx, in_=xst, axis=AX.X),
            lambda v: v.tensor_scalar_mul(out=mean, in0=sumx, scalar1=1.0 / D),
            lambda v: v.tensor_mul(out=msq, in0=mean, in1=mean),
        ])
        if tt == 0:
            mark_phase("m_stats")
        SE.wait(tok_x)
        SE.wait(tok_sqf_g)
        SE.wait(tok_tnsq)
        tok_sq = SE.emit(lambda s: s.activation(out=xsq, in_=xst, func=AF.Square,
                                                accum_out=sumsq))
        if tt == 0:
            mark_phase("m_sq")
        V.wait(tok_sq)

        tok_var = V.seq([
            lambda v: v.scalar_tensor_tensor(out=varep, in0=sumsq, scalar=1.0 / D, in1=msq,
                                             op0=ALU.mult, op1=ALU.subtract),
            lambda v: v.tensor_scalar_add(out=varep, in0=varep, scalar1=EPS),
        ])
        if tt == 0:
            mark_phase("m_var")
        SE.wait(tok_var)
        tok_sd = SE.emit(lambda s: s.activation(out=sd, in_=varep, func=AF.Sqrt))
        if tt == 0:
            mark_phase("m_sd")
        V.wait(tok_sd)

        tok_ln = V.seq([
            lambda v: v.reciprocal(out=r0n, in_=sd),
            lambda v: v.tensor_mul(out=hhn, in0=r0n, in1=r0n),
            lambda v: v.tensor_mul(out=hhn, in0=hhn, in1=varep),
            lambda v: v.tensor_scalar(out=f1n, in0=hhn, scalar1=-0.5, scalar2=1.5,
                                      op0=ALU.mult, op1=ALU.add),
            lambda v: v.tensor_mul(out=rstd, in0=r0n, in1=f1n),
            lambda v: v.tensor_scalar(out=rxF[:, 0:1024], in0=xst, scalar1=mean,
                                      scalar2=None, op0=ALU.subtract),
            lambda v: v.tensor_scalar(out=xn_tok, in0=rxF[:, 0:1024], scalar1=rstd,
                                      scalar2=None, op0=ALU.mult),
        ])
        if tt == 0:
            mark_phase("m_ln")
        TE.wait(tok_ln)
        TE.wait(tok_ev)
        TE.wait(tok_sqf)

        def t_tr(t):
            for ch in range(NDC):
                last = t.transpose(out=PSB[0][:, ch * 128:(ch + 1) * 128],
                                   in_=xn_tok[:, ch * 128:(ch + 1) * 128],
                                   identity=idnb[:, :])
            return last
        tok_tr = TE.emit(t_tr)
        if tt == 0:
            mark_phase("m_tr")
        V.wait(tok_tr)
        def v_ev(v, _col=col):
            for ch in range(NDC):
                last = v.tensor_copy(out=xnT[:, ch * T + _col:ch * T + _col + 128],
                                     in_=PSB[0][:, ch * 128:(ch + 1) * 128])
            return last
        tok_ev = V.emit(v_ev)
        if tt == 0:
            mark_phase("m_ev")
        SE.wait(tok_tr)
        SE.wait(tok_ev)
        tok_sqf = SE.emit(lambda s: s.activation(out=xsq, in_=PSB[0][:, :], func=AF.Square))
        if tt == 0:
            mark_phase("m_sqf")
        V.wait(tok_sqf)

        tok_sqf_g = V.seq([
            lambda v: v.tensor_reduce(out=tmp1, in_=xsq, axis=AX.X, op=ALU.max),
            lambda v: v.tensor_max(out=g1acc[:, :], in0=g1acc[:, :], in1=tmp1),
        ])
        if tt == 0:
            mark_phase("m_g1")
        TE.wait(tok_sqf)
        TE.wait(tok_nsqd)

        def t_nsq(t):
            for ch in range(NDC):
                last = t.matmul(PS[1][0:16, 0:128], inds[:, 16 * ch:16 * ch + 16],
                                xsq[:, ch * 128:(ch + 1) * 128],
                                start=(ch == 0), stop=(ch == NDC - 1))
            return last
        tok_tnsq = TE.emit(t_nsq)
        if tt == 0:
            mark_phase("m_nsq")
        V.wait(tok_tnsq)

        tok_nsqd = V.seq([
            lambda v: v.tensor_reduce(out=tmp1[0:16, :], in_=PS[1][0:16, 0:128],
                                      axis=AX.X, op=ALU.max),
            lambda v, _b=b: v.tensor_max(out=mbh[0:16, _b:_b + 1],
                                         in0=mbh[0:16, _b:_b + 1], in1=tmp1[0:16, :]),
        ])
    p0_ln_last = tok_ln
    p0_tr_last = tok_tr
    p0_sqf_last = tok_sqf
    p0_sqfg_last = tok_sqf_g
    p0_tnsq_last = tok_tnsq
    p0_nsq_last = tok_nsqd
    p0_ev_last = tok_ev

    def allreduce(idx, fill_tok, post_v_fn, post_bcasts):
        Q.wait(fill_tok)
        tw = Q.dma(lambda q: q.dma_start(out=arin[idx][:, :], in_=arsb[:, :]))
        G.wait(tw)
        tcc = G.emit(lambda g: g.collective_compute(
            "AllReduce", ALU.max, replica_groups=[list(range(NCORES))],
            ins=[arin[idx][:, :].opt()], outs=[arout[idx][:, :].opt()]))
        Q.wait(tcc)
        tr = Q.dma(lambda q: q.dma_start(out=arsb[:, :], in_=arout[idx][:, :]))
        return tr

    ps2 = {"tok": None}

    def part_max(acc_ap, slot, prod_tok):
        TE.wait(prod_tok)
        TE.wait(ps2["tok"])
        tok_t = TE.emit(lambda t, _a=acc_ap: t.transpose(
            out=PSB[2][0:1, 0:128], in_=_a, identity=idnb[:, :]))
        V.wait(tok_t)
        tok_v = V.emit(lambda v, _s=slot: v.tensor_reduce(
            out=arsb[0:1, _s:_s + 1], in_=PSB[2][0:1, 0:128], axis=AX.X, op=ALU.max))
        ps2["tok"] = tok_v
        return tok_v

    def bcast(dst_cols, src_lo, width, src_prod_tok):
        V.wait(src_prod_tok)

        tok_sp = V.seq([
            lambda v: v.tensor_copy(out=smB[0:1, 0:width], in_=sm1[0:1, src_lo:src_lo + width]),
            lambda v: v.tensor_sub(out=sm1[0:1, 56:56 + width],
                                   in0=sm1[0:1, src_lo:src_lo + width], in1=smB[0:1, 0:width]),
            lambda v: v.tensor_copy(out=smB[0:1, 32:32 + width], in_=sm1[0:1, 56:56 + width]),
        ])
        TE.wait(tok_sp)
        TE.wait(ps2["tok"])

        def t_bc(t):
            t.matmul(PS[2][:, 0:width], onesb[0:1, :], smB[0:1, 0:width],
                     start=True, stop=False)
            return t.matmul(PS[2][:, 0:width], onesb[0:1, :], smB[0:1, 32:32 + width],
                            start=False, stop=True)
        tok_bc = TE.emit(t_bc)
        V.wait(tok_bc)
        tok_cp = V.emit(lambda v: v.tensor_copy(out=scl[:, dst_cols], in_=PS[2][:, 0:width]))
        ps2["tok"] = tok_cp
        return tok_cp

    mark_phase("p0")
    tok_c1 = part_max(g1acc[:, :], 0, p0_sqfg_last)
    tok_r1 = allreduce(0, tok_c1, None, None)
    SE.wait(tok_r1)
    tok_g1s = SE.emit(lambda s: s.activation(out=sm1[0:1, 2:3], in_=arsb[0:1, 0:1],
                                             func=AF.Sqrt))
    V.wait(tok_g1s)

    tok_s1 = V.seq([
        lambda v: v.tensor_scalar(out=sm1[0:1, 0:1], in0=sm1[0:1, 2:3], scalar1=1.0 / QMAX,
                                  scalar2=1e-12, op0=ALU.mult, op1=ALU.max),
        lambda v: v.reciprocal(out=sm1[0:1, 1:2], in_=sm1[0:1, 0:1]),
    ])
    tok_bc1 = bcast(slice(SCL_RS1, SCL_RS1 + 1), 1, 1, tok_s1)
    TE.wait(p0_nsq_last)
    TE.wait(ps2["tok"])

    def t_mbh(t):
        t.transpose(out=PSB[2][0:1, 0:16], in_=mbh[0:16, 0:1], identity=idnb[0:16, 0:16])
        return t.transpose(out=PSB[2][0:1, 16:32], in_=mbh[0:16, 1:2],
                           identity=idnb[0:16, 0:16])
    tok_mt = TE.emit(t_mbh)
    V.wait(tok_mt)
    tok_mcp = V.emit(lambda v: v.tensor_copy(out=smB[0:1, 0:32], in_=PSB[2][0:1, 0:32]))
    TE.wait(tok_mcp)
    tok_mbc = TE.emit(lambda t: t.matmul(PS[2][:, 0:32], onesb[0:1, :], smB[0:1, 0:32],
                                         start=True, stop=True))
    V.wait(tok_mbc)
    tok_bias = V.emit(lambda v: v.tensor_scalar_mul(out=bias_all[:, :],
                                                    in0=PS[2][:, 0:32], scalar1=-0.125))
    ps2["tok"] = tok_bias

    mark_phase("ar1")
    tok_vvb = None
    for tt in range(NT):
        b, t0 = tt // 16, (tt % 16) * 128
        col = b * S + t0
        if tt == 0:
            TE.wait(p0_ev_last)
            TE.wait(p0_sqf_last)
        TE.wait(tok_vvb)

        def t_vb(t, _col=col):
            for ch in range(NDC):
                last = t.transpose(out=PSB[0][:, ch * 128:(ch + 1) * 128],
                                   in_=xnT3[:, ch, _col:_col + 128],
                                   identity=idnb[:, :])
            return last
        tok_tvb = TE.emit(t_vb)
        V.wait(tok_tvb)
        tok_vvb = V.emit(lambda v, _tt=tt: v.tensor_copy(
            out=vbuf[:, _tt * 1040:(_tt + 1) * 1040].rearrange("p (h c) -> p h c", c=65)[:, :, 0:64],
            in_=PSB[0][:, :].rearrange("p (h c) -> p h c", c=64)))
    vbuf_done = tok_vvb

    mark_phase("p1a")
    rxt = scr[:, 1024:1536]
    rxi = scrI[:, 1536:2048]
    rxc = scr[:, 0:512]
    tok_rx = None
    for k in range(NDC * NTK):
        cols = slice(k * 512, (k + 1) * 512)
        SE.wait(tok_rx)
        if k == 0:
            SE.wait(tok_bc1)
            SE.wait(p0_tr_last)
            SE.wait(p0_tnsq_last)
            SE.wait(p0_sqfg_last)
        tok_pr = SE.emit(lambda s, _c=cols: s.activation(
            out=rxt, in_=xnT[:, _c], func=AF.Copy,
            scale=scl[:, SCL_RS1:SCL_RS1 + 1], bias=RSH))
        V.wait(tok_pr)

        tok_rx = V.seq([
            lambda v: v.tensor_copy(out=rxi, in_=rxt),
            lambda v: v.scalar_tensor_tensor(out=rxc, in0=rxt, scalar=-0.5, in1=rxi,
                                             op0=ALU.add, op1=ALU.is_ge),
            lambda v, _c=cols: v.scalar_tensor_tensor(out=rxb[:, _c], in0=rxi, scalar=-RSH,
                                                      in1=rxc, op0=ALU.add, op1=ALU.add),
        ])
    rx_all = tok_rx

    mark_phase("p1b")
    wraw = scr[:, 0:1024]
    wstg = scr[:, 1024:1536]
    spare_bf = scrB[:, 3072:3584]
    ust = [scrB[:, 3584 + 512 * p:3584 + 512 * p + 512] for p in range(2)]

    def emit_dequant(Vq, wraw_ap, wraw_apI, wstg_ap, spare_ap, wdq_halves):
        tok_half = None
        for h in range(2):
            hw3 = wraw_ap[:, h * 512:(h + 1) * 512].rearrange("p (a c) -> p a c", c=32)
            iv = wraw_apI[:, 512 * h:512 * h + 512]
            wdq = wdq_halves[h]

            tok_half = Vq.seq([
                lambda v, _hw3=hw3: v.tensor_reduce(
                    out=wsc[:, :].rearrange("p (a c) -> p a c", c=1), in_=_hw3,
                    axis=AX.X, op=ALU.max, apply_absolute_value=True),
                lambda v: v.tensor_scalar(out=wss[:, :], in0=wsc[:, :], scalar1=1.0 / 7.0,
                                          scalar2=1e-12, op0=ALU.mult, op1=ALU.max),
                lambda v: v.reciprocal(out=wsc[:, :], in_=wss[:, :]),
                lambda v, _hw3=hw3, _ws=wstg_ap: v.tensor_tensor(
                    out=_ws.rearrange("p (a c) -> p a c", c=32), in0=_hw3,
                    in1=wsc[:, :].rearrange("p (a c) -> p a c", c=1).broadcast_to([128, 16, 32]),
                    op=ALU.mult),
                lambda v, _ws=wstg_ap: v.tensor_scalar_add(out=_ws, in0=_ws, scalar1=8.0),
                lambda v, _iv=iv, _ws=wstg_ap: v.tensor_copy(out=_iv, in_=_ws),
                lambda v, _iv=iv, _ws=wstg_ap: v.scalar_tensor_tensor(
                    out=_ws, in0=_ws, scalar=-0.5, in1=_iv, op0=ALU.add, op1=ALU.is_ge),
                lambda v, _iv=iv, _ws=wstg_ap, _sp=spare_ap: v.scalar_tensor_tensor(
                    out=_sp, in0=_iv, scalar=-8.0, in1=_ws, op0=ALU.add, op1=ALU.add),
                lambda v, _wdq=wdq, _sp=spare_ap: v.tensor_tensor(
                    out=_wdq.rearrange("p (a c) -> p a c", c=32),
                    in0=_sp.rearrange("p (a c) -> p a c", c=32),
                    in1=wss[:, :].rearrange("p (a c) -> p a c", c=1).broadcast_to([128, 16, 32]),
                    op=ALU.mult),
            ])
        return tok_half

    wdq_h_p1 = [scrB[:, 0:512], scrB[:, 1024:1536]]
    seq = [(w, ic) for w in ["wu", "wg"] for ic in range(NIC)]
    evac_tok = [None, None]
    stage_free = [None, None]
    wt_free = [None, None]
    wraw_ok = rx_all
    tok_mm = None
    tok_vm = None
    tok_spill = None
    tok_wtev_prev = None
    first_mm = True
    for n, (w, ic) in enumerate(seq):
        par = n % 2
        Q.wait(wraw_ok)
        tok_w = Q.dma(lambda q, _w=w, _ic=ic: q.dma_start(
            out=wraw, in_=w_in[_w][_ic * 128:(_ic + 1) * 128, :]))
        V.wait(tok_w)
        if n == 0:
            V.wait(p0_tr_last)
        tok_half = emit_dequant(V, wraw, scrI, wstg, spare_bf, wdq_h_p1)
        wraw_ok = tok_half
        TE.wait(tok_half)
        TE.wait(wt_free[par])
        TE.wait(tok_wtev_prev if tok_wtev_prev is not None else ps2["tok"])

        def t_wt(t):
            for h in range(2):
                for k2 in range(4):
                    last = t.transpose(
                        out=PSB[2][:, (4 * h + k2) * 128:(4 * h + k2 + 1) * 128],
                        in_=scrB[:, 1024 * h + k2 * 128:1024 * h + (k2 + 1) * 128],
                        identity=idnb[:, :])
            return last
        tok_twt = TE.emit(t_wt)
        V.wait(tok_twt)
        tok_wtev = V.emit(lambda v, _par=par: v.tensor_copy(
            out=wt[:, _par * 1024:(_par + 1) * 1024], in_=PSB[2][:, :]))
        tok_wtev_prev = tok_wtev
        sp_t = sp["U"] if w == "wu" else sp["G"]
        mxt = mx["U"] if w == "wu" else mx["G"]
        TE.wait(tok_wtev)
        for tk in range(NTK):
            it = n * NTK + tk
            pp = it % 2
            if first_mm:
                TE.wait(vbuf_done)
                TE.wait(p0_nsq_last)
                first_mm = False
            TE.wait(evac_tok[pp])

            def t_mm1(t, _par=par, _tk=tk, _pp=pp):
                for dc in range(NDC):
                    last = t.matmul(PS[_pp][:, :],
                                    wt[:, _par * 1024 + dc * 128:_par * 1024 + (dc + 1) * 128],
                                    rxb[:, dc * T + _tk * 512:dc * T + (_tk + 1) * 512],
                                    start=(dc == 0), stop=(dc == NDC - 1))
                return last
            tok_mm = TE.emit(t_mm1)
            up = it % 2
            SE.wait(tok_mm)
            SE.wait(stage_free[up])
            tok_evs = SE.emit(lambda s, _pp=pp, _up=up: s.activation(
                out=ust[_up], in_=PS[_pp][:, :], func=AF.Copy))
            evac_tok[pp] = tok_evs
            V.wait(tok_evs)

            tok_vm = V.seq([
                lambda v, _up=up: v.tensor_reduce(out=tmp1, in_=ust[_up], axis=AX.X,
                                                  op=ALU.max, apply_absolute_value=True),
                lambda v, _mx=mxt: v.tensor_max(out=_mx[:, :], in0=_mx[:, :], in1=tmp1),
            ])
            Q.wait(tok_vm)
            tok_spill = Q.dma(lambda q, _sp=sp_t, _ic=ic, _tk=tk, _up=up: q.dma_start(
                out=_sp[_ic, _tk, :, :], in_=ust[_up]))
            stage_free[up] = tok_spill
        wt_free[par] = tok_mm
    upgate_t_done = tok_mm
    upgate_v_done = tok_vm
    upgate_spill_done = tok_spill

    mark_phase("p1c")
    wdraw = rxF[:, 12288:13312]
    wdrawI = rxI[:, 12288:13312]
    wdstg = rxF[:, 13312:13824]
    wd_spare = rxb[:, 27648:28160]
    wdq_h_wd = [rxb[:, 28672:29184], rxb[:, 29184:29696]]
    wd_prev_dma = wraw_ok
    tok_wdev = None
    for dc in range(NDC):
        for sc3 in range(3):
            Q.wait(wd_prev_dma)
            if dc == 0 and sc3 == 0:
                Q.wait(upgate_t_done)
            tok_w = Q.dma(lambda q, _dc=dc, _s=sc3: q.dma_start(
                out=wdraw, in_=wd_in[_dc * 128:(_dc + 1) * 128, _s * 1024:(_s + 1) * 1024]))
            V.wait(tok_w)
            if dc == 0 and sc3 == 0:
                V.wait(upgate_t_done)
            tok_half = emit_dequant(V, wdraw, wdrawI, wdstg, wd_spare, wdq_h_wd)
            wd_prev_dma = tok_half
            TE.wait(tok_half)
            TE.wait(tok_wdev if tok_wdev is not None else tok_wtev_prev)

            def t_wd(t):
                for h in range(2):
                    for k2 in range(4):
                        last = t.transpose(
                            out=PSB[2][:, (4 * h + k2) * 128:(4 * h + k2 + 1) * 128],
                            in_=wdq_h_wd[h][:, k2 * 128:(k2 + 1) * 128],
                            identity=idnb[:, :])
                return last
            tok_twd = TE.emit(t_wd)
            V.wait(tok_twd)
            ic0 = sc3 * 8

            def v_wdev(v, _ic0=ic0, _dc=dc):
                return v.tensor_copy(
                    out=rxb[:, :].rearrange("p (ic x) -> p ic x", x=1024)[:, _ic0:_ic0 + 8,
                                                                          _dc * 128:(_dc + 1) * 128],
                    in_=PSB[2][:, :].rearrange("p (a c) -> p a c", c=128))
            tok_wdev = V.emit(v_wdev)
    wd_ready = tok_wdev

    mark_phase("wd")
    ps2["tok"] = wd_ready if ps2["tok"] is None else ps2["tok"]
    TE.wait(wd_ready)
    part_max(mx["U"][:, :], 1, upgate_v_done)
    tok_c2b = part_max(mx["G"][:, :], 2, upgate_v_done)
    tok_r2 = allreduce(1, tok_c2b, None, None)
    V.wait(tok_r2)

    tok_s23 = V.seq([
        lambda v: v.scalar_tensor_tensor(out=sm1[0:1, 4:5], in0=arsb[0:1, 1:2], scalar=1.0,
                                         in1=sm1[0:1, 0:1], op0=ALU.bypass, op1=ALU.mult),
        lambda v: v.tensor_scalar(out=sm1[0:1, 4:5], in0=sm1[0:1, 4:5], scalar1=1.0 / QMAX,
                                  scalar2=1e-12, op0=ALU.mult, op1=ALU.max),
        lambda v: v.reciprocal(out=sm1[0:1, 5:6], in_=sm1[0:1, 4:5]),
        lambda v: v.scalar_tensor_tensor(out=sm1[0:1, 7:8], in0=arsb[0:1, 2:3], scalar=1.0,
                                         in1=sm1[0:1, 0:1], op0=ALU.bypass, op1=ALU.mult),
        lambda v: v.tensor_scalar(out=sm1[0:1, 7:8], in0=sm1[0:1, 7:8], scalar1=1.0 / QMAX,
                                  scalar2=1e-12, op0=ALU.mult, op1=ALU.max),
        lambda v: v.reciprocal(out=sm1[0:1, 41:42], in_=sm1[0:1, 7:8]),
        lambda v: v.scalar_tensor_tensor(out=sm1[0:1, 51:52], in0=sm1[0:1, 5:6], scalar=1.0,
                                         in1=sm1[0:1, 0:1], op0=ALU.bypass, op1=ALU.mult),
        lambda v: v.scalar_tensor_tensor(out=sm1[0:1, 52:53], in0=sm1[0:1, 41:42], scalar=1.0,
                                         in1=sm1[0:1, 0:1], op0=ALU.bypass, op1=ALU.mult),
        lambda v: v.tensor_copy(out=sm1[0:1, 53:54], in_=sm1[0:1, 7:8]),
    ])
    scales2_tok = bcast(slice(SCL_AU, SCL_S3 + 1), 51, 3, tok_s23)

    mark_phase("ar2")
    slab = [[scrB[:, (2 * e + par) * 512:(2 * e + par) * 512 + 512] for par in range(2)]
            for e in range(2)]
    sstg = [scrB[:, 2048 + 512 * e:2048 + 512 * e + 512] for e in range(2)]
    sstg2 = [scrB[:, 3072 + 256 * e:3072 + 256 * e + 256] for e in range(2)]
    SCB = [[0, 1], [3, 4]]
    PVB = [5, 6]
    P2B = 7

    rbs = [rxb[:, 24576 + 512 * k:24576 + 512 * (k + 1)] for k in range(4)]
    p2t = rxF[:, 13312:13824]
    p2i = rxI[:, 13824:14336]
    p2c = rxF[:, 14336:14848]
    rgs = rxb[:, 29696:30208]
    rss = rxb[:, 30208:30720]
    m1s = rxb[:, 30720:31232]
    rup = rxb[:, 31232:31744]
    sgs = rxb[:, 31744:32256]

    att = {"slab_free": {}, "sc_free": {}, "pv_free": [None, None],
           "p2_free": None, "sstg_free": [None, None], "sstg2_free": [None, None]}
    first_exp = True
    first_sc = True

    def emit_attention_iter(b, hp, j):
        nonlocal first_exp, first_sc
        C = 4 * j + 4
        base = hp * T + b * S
        exp_toks = {}
        pv_last = {}
        for c in range(C):
            off = max(0, 128 * (c - 4 * j))
            N = 512 - off
            mm_toks = {}
            for e in range(2):
                bank = SCB[e][c % 2]
                if first_sc:
                    TE.wait(evac_tok[0])
                    TE.wait(evac_tok[1])
                    first_sc = False
                TE.wait(att["sc_free"].get((e, c % 2)))

                def t_sc(t, _e=e, _bank=bank, _c=c, _N=N, _base=base, _j=j, _off=off):
                    return t.matmul(PS[_bank][:, 0:_N],
                                    xnT[64 * _e:64 * _e + 64,
                                        _base + _c * 128:_base + (_c + 1) * 128],
                                    xnT[64 * _e:64 * _e + 64,
                                        _base + 512 * _j + _off:_base + 512 * (_j + 1)],
                                    start=True, stop=True, tile_position=(64 * _e, 0))
                mm_toks[e] = TE.emit(t_sc)
            for e in range(2):
                SE.wait(mm_toks[e])
                SE.wait(att["slab_free"].get((e, c % 2)))
                if first_exp:
                    SE.wait(tok_bias)
                    first_exp = False
                hidx = b * 16 + 2 * hp + e
                bank = SCB[e][c % 2]
                tok_e = SE.emit(lambda s, _e=e, _c=c, _N=N, _bk=bank, _h=hidx:
                                s.activation(out=slab[_e][_c % 2][:, 0:_N],
                                             in_=PS[_bk][:, 0:_N], func=AF.Exp,
                                             scale=0.125, bias=bias_all[:, _h:_h + 1]))
                att["sc_free"][(e, c % 2)] = tok_e
                if c >= 4 * j:
                    G.wait(tok_e)
                    tok_e = G.emit(lambda g, _e=e, _c=c: g.affine_select(
                        out=slab[_e][_c % 2][:, 0:128], in_=slab[_e][_c % 2][:, 0:128],
                        pattern=[[1, 128]], compare_op=ALU.is_ge, fill=0.0,
                        base=0, channel_multiplier=-1))
                exp_toks[(e, c)] = tok_e
            for e in range(2):
                tile = b * 16 + c
                TE.wait(exp_toks[(e, c)])
                if c == 0:
                    TE.wait(att["pv_free"][e])

                def t_pv(t, _e=e, _c=c, _off=off, _N=N, _tile=tile, _hp=hp, _C=C):
                    return t.matmul(PS[PVB[_e]][0:65, _off:512],
                                    vbuf[:, _tile * 1040 + (2 * _hp + _e) * 65:
                                         _tile * 1040 + (2 * _hp + _e) * 65 + 65],
                                    slab[_e][_c % 2][:, 0:_N],
                                    start=(_c == 0), stop=(_c == _C - 1))
                tok_pv = TE.emit(t_pv)
                att["slab_free"][(e, c % 2)] = tok_pv
                pv_last[e] = tok_pv
        for e in range(2):
            V.wait(pv_last[e])
            V.wait(att["sstg_free"][e])
            tok_cp = V.emit(lambda v, _e=e: v.tensor_copy(out=sstg[_e][0:65, :],
                                                          in_=PS[PVB[_e]][0:65, :]))
            att["pv_free"][e] = tok_cp
            TE.wait(tok_cp)
            TE.wait(att["p2_free"])

            def t_n(t, _e=e):
                for blk in range(4):
                    last = t.transpose(
                        out=PSB[P2B][:, _e * 512 + blk * 66:_e * 512 + blk * 66 + 65],
                        in_=sstg[_e][0:65, blk * 128:(blk + 1) * 128],
                        identity=idnb[0:65, 0:65])
                return last
            tok_tn = TE.emit(t_n)
            att["sstg_free"][e] = tok_tn
            V.wait(tok_tn)
            V.wait(att["sstg2_free"][e])

            tok_vn = V.seq([
                lambda v, _e=e: v.reciprocal(
                    out=rd4.rearrange("p (a c) -> p a c", c=1),
                    in_=PSB[P2B][:, _e * 512 + 64:_e * 512 + 64 + 4 * 66]
                    .rearrange("p (a c) -> p a c", c=66)[:, :, 0:1]),
                lambda v, _e=e: v.tensor_tensor(
                    out=sstg2[_e].rearrange("p (a c) -> p a c", c=64),
                    in0=PSB[P2B][:, _e * 512:_e * 512 + 264]
                    .rearrange("p (a c) -> p a c", c=66)[:, :, 0:64],
                    in1=rd4.rearrange("p (a c) -> p a c", c=1).broadcast_to([128, 4, 64]),
                    op=ALU.mult),
            ])
            att["p2_free"] = tok_vn
            h = 2 * hp + e
            last_d = None
            for blk in range(4):
                if blk == 0:
                    Q.wait(tok_vn)
                last_d = Q.dma(lambda q, _e=e, _blk=blk, _h=h, _ti=b * 16 + 4 * j + blk:
                               q.dma_start(out=sout_d[_ti, :, _h * 64:(_h + 1) * 64],
                                           in_=sstg2[_e][:, _blk * 64:(_blk + 1) * 64]))
            att["sstg2_free"][e] = last_d

    p1_iters = [(ic, tk) for ic in range(NIC) for tk in range(NTK)]
    p1 = {"rb": [None, None], "last_v": None, "spill": None}

    def emit_pass1_iter(idx):
        ic, tk = p1_iters[idx]
        sl = idx % 2
        rbU, rbG = rbs[2 * sl], rbs[2 * sl + 1]
        Q.wait(p1["rb"][sl])
        if idx == 0:
            Q.wait(upgate_spill_done)
            Q.wait(wd_ready)
        d1 = Q.dma(lambda q, _ic=ic, _tk=tk, _r=rbU: q.dma_start(out=_r, in_=sp["U"][_ic, _tk, :, :]))
        d2 = Q.dma(lambda q, _ic=ic, _tk=tk, _r=rbG: q.dma_start(out=_r, in_=sp["G"][_ic, _tk, :, :]))
        SE.wait(d2)
        if idx == 0:
            SE.wait(scales2_tok)
        SE.wait(p1["last_v"])
        tU = SE.emit(lambda s, _r=rbU: s.activation(out=p2t, in_=_r, func=AF.Copy,
                                                    scale=scl[:, SCL_AU:SCL_AU + 1], bias=RSH))
        V.wait(tU)

        v1 = V.seq([
            lambda v: v.tensor_copy(out=p2i, in_=p2t),
            lambda v: v.scalar_tensor_tensor(out=p2c, in0=p2t, scalar=-0.5, in1=p2i,
                                             op0=ALU.add, op1=ALU.is_ge),
            lambda v: v.scalar_tensor_tensor(out=rup, in0=p2i, scalar=-RSH, in1=p2c,
                                             op0=ALU.add, op1=ALU.add),
        ])
        Q.wait(v1)
        Q.dma(lambda q, _ic=ic, _tk=tk: q.dma_start(out=sp["Rup"][_ic, _tk, :, :], in_=rup))
        SE.wait(v1)
        tG = SE.emit(lambda s, _r=rbG: s.activation(out=p2t, in_=_r, func=AF.Copy,
                                                    scale=scl[:, SCL_AG:SCL_AG + 1], bias=RSH))
        G.wait(tG)
        g0t = G.emit(lambda g: g.tensor_copy(out=p2i, in_=p2t))
        V.wait(g0t)

        g1t = V.seq([
            lambda v: v.scalar_tensor_tensor(out=p2c, in0=p2t, scalar=-0.5, in1=p2i,
                                             op0=ALU.add, op1=ALU.is_ge),
            lambda v: v.scalar_tensor_tensor(out=rgs, in0=p2i, scalar=-RSH, in1=p2c,
                                             op0=ALU.add, op1=ALU.add),
        ])
        SE.wait(g1t)
        tSg = SE.emit(lambda s: s.activation(out=sgs, in_=rgs, func=AF.Sigmoid,
                                             scale=scl[:, SCL_S3:SCL_S3 + 1]))
        G.wait(tSg)

        g15a = G.emit(lambda g: g.tensor_scalar_mul(out=p2t, in0=sgs, scalar1=65536.0))
        G.wait(g15a)
        g15 = G.emit(lambda g: g.tensor_copy(out=p2i, in_=p2t))
        V.wait(g15)

        v15 = V.seq([
            lambda v: v.scalar_tensor_tensor(out=p2c, in0=p2t, scalar=-0.5, in1=p2i,
                                             op0=ALU.add, op1=ALU.is_ge),
            lambda v: v.scalar_tensor_tensor(out=p2c, in0=p2i, scalar=0.0, in1=p2c,
                                             op0=ALU.add, op1=ALU.add),
        ])
        G.wait(v15)

        g2a = G.emit(lambda g: g.tensor_scalar_min(out=rss, in0=p2c, scalar1=65535.0))
        G.wait(g2a)
        g2t = G.emit(lambda g: g.tensor_tensor(out=m1s, in0=rgs, in1=rss, op=ALU.mult))
        V.wait(g2t)

        v2 = V.seq([
            lambda v: v.tensor_reduce(out=tmp1, in_=m1s, axis=AX.X, op=ALU.max,
                                      apply_absolute_value=True),
            lambda v: v.tensor_max(out=mx["M1"][:, :], in0=mx["M1"][:, :], in1=tmp1),
        ])
        Q.wait(v2)
        dm1 = Q.dma(lambda q, _ic=ic, _tk=tk: q.dma_start(out=sp["M1"][_ic, _tk, :, :], in_=m1s))
        p1["rb"][sl] = tG
        p1["last_v"] = v2
        p1["spill"] = dm1

    p2st = {"rb": [None, None], "last_v": None, "spill": None}

    def emit_pass2_iter(idx, gate_tok, p1_spill_tok):
        ic, tk = p1_iters[idx]
        sl = idx % 2
        rbM, rbR = rbs[2 * sl], rbs[2 * sl + 1]
        Q.wait(p2st["rb"][sl])
        if idx == 0:
            Q.wait(p1_spill_tok)
        d1 = Q.dma(lambda q, _ic=ic, _tk=tk, _r=rbM: q.dma_start(out=_r, in_=sp["M1"][_ic, _tk, :, :]))
        d2 = Q.dma(lambda q, _ic=ic, _tk=tk, _r=rbR: q.dma_start(out=_r, in_=sp["Rup"][_ic, _tk, :, :]))
        SE.wait(d2)
        if idx == 0:
            SE.wait(gate_tok)
        SE.wait(p2st["last_v"])
        tM = SE.emit(lambda s, _r=rbM: s.activation(out=p2t, in_=_r, func=AF.Copy,
                                                    scale=scl[:, SCL_AM:SCL_AM + 1], bias=RSH))
        V.wait(tM)

        v2 = V.seq([
            lambda v: v.tensor_copy(out=p2i, in_=p2t),
            lambda v: v.scalar_tensor_tensor(out=p2c, in0=p2t, scalar=-0.5, in1=p2i,
                                             op0=ALU.add, op1=ALU.is_ge),
            lambda v: v.scalar_tensor_tensor(out=rgs, in0=p2i, scalar=-RSH, in1=p2c,
                                             op0=ALU.add, op1=ALU.add),
            lambda v, _r=rbR: v.tensor_mul(out=m1s, in0=rgs, in1=_r),
            lambda v: v.tensor_reduce(out=tmp1, in_=m1s, axis=AX.X, op=ALU.max,
                                      apply_absolute_value=True),
            lambda v: v.tensor_max(out=mx["Uu"][:, :], in0=mx["Uu"][:, :], in1=tmp1),
        ])
        Q.wait(v2)
        du = Q.dma(lambda q, _ic=ic, _tk=tk: q.dma_start(out=sp["Uu"][_ic, _tk, :, :], in_=m1s))
        p2st["rb"][sl] = v2
        p2st["last_v"] = v2
        p2st["spill"] = du

    ar3_gate = None
    p1_idx = 0
    p2_idx = 0
    p1_spill_last = None
    atts = [(b, hp) for b in range(2) for hp in range(8)]
    for ai, (b, hp) in enumerate(atts):
        for j in range(4):
            emit_attention_iter(b, hp, j)
        if ai < 10:
            for _ in range(min(20, 192 - p1_idx)):
                emit_pass1_iter(p1_idx)
                p1_idx += 1
        if ai == 9:
            assert p1_idx == 192
            p1_spill_last = p1["spill"]
            tok_c3 = part_max(mx["M1"][:, :], 3, p1["last_v"])
            tok_r3 = allreduce(2, tok_c3, None, None)
            V.wait(tok_r3)

            tok_s4 = V.seq([
                lambda v: v.scalar_tensor_tensor(out=sm1[0:1, 42:43], in0=arsb[0:1, 3:4],
                                                 scalar=1.0 / 65536.0, in1=sm1[0:1, 7:8],
                                                 op0=ALU.mult, op1=ALU.mult),
                lambda v: v.tensor_scalar(out=sm1[0:1, 43:44], in0=sm1[0:1, 42:43],
                                          scalar1=1.0 / QMAX, scalar2=1e-12,
                                          op0=ALU.mult, op1=ALU.max),
                lambda v: v.reciprocal(out=sm1[0:1, 44:45], in_=sm1[0:1, 43:44]),
                lambda v: v.scalar_tensor_tensor(out=sm1[0:1, 54:55], in0=sm1[0:1, 44:45],
                                                 scalar=1.0 / 65536.0, in1=sm1[0:1, 7:8],
                                                 op0=ALU.mult, op1=ALU.mult),
            ])
            ar3_gate = bcast(slice(SCL_AM, SCL_AM + 1), 54, 1, tok_s4)
        if ai >= 10:
            for _ in range(min(32, 192 - p2_idx)):
                emit_pass2_iter(p2_idx, ar3_gate, p1_spill_last)
                p2_idx += 1
    assert p2_idx == 192

    mark_phase("p2")
    tok_c4 = part_max(mx["Uu"][:, :], 4, p2st["last_v"])
    tok_r4 = allreduce(3, tok_c4, None, None)
    V.wait(tok_r4)

    tok_s5 = V.seq([
        lambda v: v.scalar_tensor_tensor(out=sm1[0:1, 46:47], in0=sm1[0:1, 43:44], scalar=1.0,
                                         in1=sm1[0:1, 4:5], op0=ALU.bypass, op1=ALU.mult),
        lambda v: v.scalar_tensor_tensor(out=sm1[0:1, 47:48], in0=arsb[0:1, 4:5], scalar=1.0,
                                         in1=sm1[0:1, 46:47], op0=ALU.bypass, op1=ALU.mult),
        lambda v: v.tensor_scalar(out=sm1[0:1, 48:49], in0=sm1[0:1, 47:48], scalar1=1.0 / QMAX,
                                  scalar2=1e-12, op0=ALU.mult, op1=ALU.max),
        lambda v: v.reciprocal(out=sm1[0:1, 49:50], in_=sm1[0:1, 48:49]),
        lambda v: v.scalar_tensor_tensor(out=sm1[0:1, 55:56], in0=sm1[0:1, 49:50], scalar=1.0,
                                         in1=sm1[0:1, 46:47], op0=ALU.bypass, op1=ALU.mult),
        lambda v: v.tensor_copy(out=sm1[0:1, 56:57], in_=sm1[0:1, 48:49]),
    ])
    scales5_tok = bcast(slice(SCL_AO, SCL_S5 + 1), 55, 2, tok_s5)

    mark_phase("ar4")
    xrd = vbF[:, 0:1024]
    ostg = vbF[:, 1024:2048]
    sback = vbuf[:, 4096:5120]
    gall = [vbuf[:, 5120 + 512 * dc:5120 + 512 * (dc + 1)] for dc in range(NDC)]
    ro = [rbs[1], rbs[3]]

    dn = {"rb": [None, None], "ro_free": [None, None], "last_v": None}
    prev_ps_users = [att["sc_free"].get((0, 0)), att["sc_free"].get((0, 1)), scales5_tok,
                     att["sc_free"].get((1, 0)), att["sc_free"].get((1, 1)),
                     att["pv_free"][0], att["pv_free"][1], att["p2_free"]]
    tok_dn_evac_prev = [None] * NDC
    tok_comb_prev = None
    tok_tg_prev = None
    tok_out_last = None
    tok_cb_prev = None
    p3_first = True
    tok_dn = None
    for tk in range(NTK):
        for ic in range(NIC):
            sl = ic % 2
            Q.wait(dn["rb"][sl])
            if p3_first:
                Q.wait(p2st["spill"])
            d1 = Q.dma(lambda q, _ic=ic, _tk=tk, _r=rbs[2 * sl]:
                       q.dma_start(out=_r, in_=sp["Uu"][_ic, _tk, :, :]))
            SE.wait(d1)
            if p3_first:
                SE.wait(scales5_tok)
            SE.wait(dn["last_v"])
            tP = SE.emit(lambda s, _r=rbs[2 * sl]: s.activation(
                out=p2t, in_=_r, func=AF.Copy, scale=scl[:, SCL_AO:SCL_AO + 1], bias=RSH))
            dn["rb"][sl] = tP
            V.wait(tP)
            V.wait(dn["ro_free"][sl])

            vro = V.seq([
                lambda v: v.tensor_copy(out=p2i, in_=p2t),
                lambda v: v.scalar_tensor_tensor(out=p2c, in0=p2t, scalar=-0.5, in1=p2i,
                                                 op0=ALU.add, op1=ALU.is_ge),
                lambda v, _sl=sl: v.scalar_tensor_tensor(out=ro[_sl], in0=p2i, scalar=-RSH,
                                                         in1=p2c, op0=ALU.add, op1=ALU.add),
            ])
            dn["last_v"] = vro
            p3_first = False
            TE.wait(vro)
            if ic == 0:
                for dc in range(NDC):
                    TE.wait(prev_ps_users[dc] if tk == 0 else tok_dn_evac_prev[dc])
                if tk > 0:
                    TE.wait(tok_cb_prev)

            def t_dn(t, _sl=sl, _ic=ic):
                for dc in range(NDC):
                    last = t.matmul(PS[dc][:, :],
                                    rxb[:, _ic * 1024 + dc * 128:_ic * 1024 + (dc + 1) * 128],
                                    ro[_sl], start=(_ic == 0), stop=(_ic == NIC - 1))
                return last
            tok_dn = TE.emit(t_dn)
            dn["ro_free"][sl] = tok_dn
        tok_si = None
        for dc in range(NDC):
            SE.wait(tok_dn if dc == 0 else None)
            SE.wait(tok_tg_prev if dc == 0 else None)
            if tk == 0 and dc == 0:
                SE.wait(vbuf_done)
            tok_si = SE.emit(lambda s, _dc=dc: s.activation(
                out=gall[_dc], in_=PS[_dc][:, :], func=AF.Silu, scale=scl[:, SCL_S5:SCL_S5 + 1]))
            tok_dn_evac_prev[dc] = tok_si
        for tl in range(4):
            tt = tk * 4 + tl
            b, t0 = tt // 16, (tt % 16) * 128
            Q.wait(tok_comb_prev)
            if tt == 0:
                Q.wait(att["sstg2_free"][0])
                Q.wait(att["sstg2_free"][1])
                Q.wait(tok_dn)
            dx = Q.dma(lambda q, _b=b, _t0=t0: q.dma_start(out=xrd, in_=xin[_b, _t0:_t0 + 128, :]))
            ds = Q.dma(lambda q, _tt=tt: q.dma_start(out=sback, in_=sout_d[_tt, :, :]))
            TE.wait(tok_si)
            TE.wait(tok_comb_prev)

            def t_g(t, _tl=tl):
                for dc in range(NDC):
                    last = t.transpose(out=PSB[0][:, dc * 128:(dc + 1) * 128],
                                       in_=gall[dc][:, _tl * 128:(_tl + 1) * 128],
                                       identity=idnb[:, :])
                return last
            tok_tg = TE.emit(t_g)
            tok_tg_prev = tok_tg
            V.wait(tok_tg)
            V.wait(ds)
            V.wait(dx)

            tok_cb = V.seq([
                lambda v: v.scalar_tensor_tensor(out=ostg, in0=sback, scalar=1.0,
                                                 in1=PSB[0][:, :], op0=ALU.bypass, op1=ALU.mult),
                lambda v: v.tensor_add(out=ostg, in0=ostg, in1=xrd),
            ])
            tok_cb_prev = tok_cb
            Q.wait(tok_cb)
            tok_out_last = Q.dma(lambda q, _b=b, _t0=t0: q.dma_start(
                out=out_d[_b, _t0:_t0 + 128, :], in_=ostg))
            tok_comb_prev = tok_out_last

    if cut is not None:
        snap = PH[cut]
        for k, e in engs.items():
            e.ops = e.ops[:snap[k][0]]
            e.n = snap[k][1]
    if dbg_out:
        pairs = {"xnT": xnT[:, :], "vbuf": vbuf[:, :], "rx": rxb[:, :],
                 "bias": bias_all[:, :], "scales": scl[:, :],
                 "wdt": rxb[:, 0:NIC * 1024], "sm1": sm1[:, :]}
        Q.wait(("v", V.n))
        Q.wait(("s", SE.n))
        Q.wait(("g", G.n))
        Q.wait(("t", TE.n))
        for nm in dbg_out:
            Q.dma(lambda q, _d=dbg_out[nm], _s=pairs[nm]: q.dma_start(out=_d[:, :], in_=_s))

    Q.ops.append(lambda q, _v=Q.n: q.wait_ge(_SEMH["q"], _v))

    with nc.Block() as block:
        @block.sync
        def _(sync):
            for fn in Q.ops:
                fn(sync)

        @block.tensor
        def _(tensor):
            for fn in TE.ops:
                fn(tensor)

        @block.vector
        def _(vector):
            for fn in V.ops:
                fn(vector)

        @block.scalar
        def _(scalar):
            for fn in SE.ops:
                fn(scalar)

        @block.gpsimd
        def _(gpsimd):
            for fn in G.ops:
                fn(gpsimd)

    es.close()
    return nc, dbg_out


def kernel(L0_attn_x, L0_ffn_w_gate, L0_ffn_w_up, L0_ffn_w_down):
    x = np.ascontiguousarray(np.asarray(L0_attn_x, dtype=np.float32))
    wg = np.ascontiguousarray(np.asarray(L0_ffn_w_gate, dtype=np.float32))
    wu = np.ascontiguousarray(np.asarray(L0_ffn_w_up, dtype=np.float32))
    wd = np.ascontiguousarray(np.asarray(L0_ffn_w_down, dtype=np.float32))
    nc, _ = build_program()
    in_maps = [{"x": x[2 * c:2 * c + 2], "wu": wu, "wg": wg, "wd": wd}
               for c in range(NCORES)]
    res = run_bass_kernel_spmd(nc, in_maps, core_ids=list(range(NCORES)))
    return np.concatenate([r["out"] for r in res.results], axis=0)


# version 14
# speedup vs baseline: 1.3395x; 1.3395x over previous
import contextlib
import numpy as np
import ml_dtypes
import concourse.bass as bass
import concourse.bass_isa as bass_isa
import concourse.mybir as mybir
from concourse.bass_utils import run_bass_kernel_spmd

F32 = mybir.dt.float32
BF16 = mybir.dt.bfloat16
I32 = mybir.dt.int32
AF = mybir.ActivationFunctionType
ALU = mybir.AluOpType
AX = mybir.AxisListType

NCORES = 8
B, S, D, H, HD, I = 2, 2048, 1024, 16, 64, 3072
T = B * S
NT = T // 128
NDC = D // 128
NIC = I // 128
NTK = T // 512
QMAX = 32767.0
RSH = 32768.0
EPS = 1e-5
SCL_RS1, SCL_AU, SCL_AG, SCL_S3, SCL_AM, SCL_AO, SCL_S5 = 0, 1, 2, 3, 4, 5, 6

_SEMH = {}


class Eng:
    def __init__(self, key):
        self.key, self.n, self.ops = key, 0, []

    def emit(self, fn):
        self.n += 1
        tgt = self.n

        def wrapped(eng, _fn=fn, _E=self):
            _fn(eng).then_inc(_E.semh, 1)
        self.ops.append(wrapped)
        return (self.key, tgt)

    def dma(self, fn):
        self.n += 16
        tgt = self.n

        def wrapped(eng, _fn=fn, _E=self):
            _fn(eng).then_inc(_E.semh, 16)
        self.ops.append(wrapped)
        return (self.key, tgt)

    def wait(self, token):
        if token is None:
            return
        key, v = token
        self.ops.append(lambda eng, _k=key, _v=v: eng.wait_ge(_SEMH[_k], _v))

    def seq(self, fns):
        for i, fn in enumerate(fns):
            self.n += 1
            tgt = self.n

            def wrapped(eng, _fn=fn, _E=self):
                _fn(eng).then_inc(_E.semh, 1)
            self.ops.append(wrapped)
            if i + 1 < len(fns):
                self.ops.append(lambda eng, _k=self.key, _v=tgt: eng.wait_ge(_SEMH[_k], _v))
        return (self.key, self.n)


def build_program(dbg=(), cut=None):
    nc = bass.Bass("TRN2", target_bir_lowering=False, detect_race_conditions=False)
    xin = nc.dram_tensor("x", [B, S, D], F32, kind="ExternalInput")
    w_in = {"wu": nc.dram_tensor("wu", [I, D], F32, kind="ExternalInput"),
            "wg": nc.dram_tensor("wg", [I, D], F32, kind="ExternalInput")}
    wd_in = nc.dram_tensor("wd", [D, I], F32, kind="ExternalInput")
    out_d = nc.dram_tensor("out", [B, S, D], F32, kind="ExternalOutput")
    spUG = nc.dram_tensor("spUG", [NIC, NTK, 128, 2, 512], BF16)
    spMR = nc.dram_tensor("spMR", [NIC, NTK, 128, 2, 512], BF16)
    spUu = nc.dram_tensor("spUu", [NIC, NTK, 128, 512], BF16)
    sout_d = nc.dram_tensor("sout_d", [NT, 128, D], BF16)
    arin = [nc.dram_tensor(f"arin{k}", [1, 8], F32) for k in range(4)]
    arout = [nc.dram_tensor(f"arout{k}", [1, 8], F32) for k in range(4)]

    dbspec = {"xnT": ([128, NDC * T], BF16), "vbuf": ([128, NT * 1040], BF16),
              "bias": ([128, 32], F32), "scales": ([128, 8], F32),
              "rx": ([128, NDC * T], BF16), "wdt": ([128, NIC * 1024], BF16),
              "sm1": ([1, 60], F32)}
    dbg_out = {nm: nc.dram_tensor("dbg_" + nm, *dbspec[nm], kind="ExternalOutput")
               for nm in dbg}

    es = contextlib.ExitStack()
    sb = lambda n, shp, dt: es.enter_context(nc.sbuf_tensor(n, shp, dt))

    xnT = sb("xnT", [128, NDC * T], BF16)
    rxb = sb("rxb", [128, NDC * T], BF16)
    vbuf = sb("vbuf", [128, NT * 1040], BF16)
    scr = sb("scr", [128, 2304], F32)
    wt = sb("wt", [128, 2048], BF16)
    tri = sb("tri", [128, 128], BF16)
    smB = sb("smB", [1, 64], BF16)
    idnb = sb("idnb", [128, 128], BF16)
    inds = sb("inds", [128, 128], BF16)
    bias_all = sb("bias_all", [128, 32], F32)
    scl = sb("scl", [128, 8], F32)
    smf = sb("smf", [128, 20], F32)
    wsc = sb("wsc", [128, 16], F32)
    wss = sb("wss", [128, 16], F32)
    sm1 = sb("sm1", [1, 60], F32)
    g1acc = sb("g1acc", [128, 1], BF16)
    mx = {k: sb("mx" + k, [128, 1], BF16) for k in ["U", "G", "M1", "Uu"]}
    mbh = sb("mbh", [128, 2], BF16)
    arsb = sb("arsb", [1, 8], F32)

    scrB = scr.bitcast(BF16)
    scrI = scr.bitcast(I32)
    rxF = rxb.bitcast(F32)
    rxI = rxb.bitcast(I32)
    vbF = vbuf.bitcast(F32)

    PS = [es.enter_context(nc.psum_tensor(f"ps{k}", [128, 512], F32)) for k in range(8)]
    PSB = [p.bitcast(BF16) for p in PS]

    engs = {k: Eng(k) for k in ["q", "t", "v", "s", "g"]}
    for k, e in engs.items():
        e.semh = es.enter_context(nc.semaphore("sem_" + k))
        _SEMH[k] = e.semh
    Q, TE, V, SE, G = (engs[k] for k in ["q", "t", "v", "s", "g"])


    PH = {}

    def mark_phase(name):
        PH[name] = {k: (len(e.ops), e.n) for k, e in engs.items()}
    sumx, sumsq = smf[:, 0:1], smf[:, 1:2]
    mean, msq = smf[:, 2:3], smf[:, 3:4]
    varep, sd = smf[:, 4:5], smf[:, 5:6]
    r0n, hhn, f1n = smf[:, 6:7], smf[:, 7:8], smf[:, 8:9]
    rstd, negmr, tmp1 = smf[:, 9:10], smf[:, 10:11], smf[:, 11:12]
    rd4 = smf[:, 12:16]

    xnT3 = xnT[:, :].rearrange("p (c t) -> p c t", t=T)

    def g_init1(g):
        g.memset(idnb[:, :], 1.0)
        g.memset(inds[:, :], 0.0)
        g.memset(g1acc[:, :], 0.0)
        for k in mx:
            g.memset(mx[k][:, :], 0.0)
        g.memset(mbh[:, :], 0.0)
        g.memset(arsb[:, :], 0.0)
        return g.memset(vbuf[:, :].rearrange("p (a c) -> p a c", c=65)[:, :, 64:65], 1.0)
    t_i1 = G.emit(g_init1)
    G.wait(t_i1)
    t_i2 = G.emit(lambda g: g.affine_select(
        out=idnb[:, :], in_=idnb[:, :], pattern=[[1, 128]],
        compare_op=ALU.is_equal, fill=0.0, base=0, channel_multiplier=-1))
    G.wait(t_i2)
    t_i2b = G.emit(lambda g: g.memset(tri[:, :], 1.0))
    G.wait(t_i2b)
    t_i2c = G.emit(lambda g: g.affine_select(
        out=tri[:, :], in_=tri[:, :], pattern=[[1, 128]],
        compare_op=ALU.is_ge, fill=0.0, base=0, channel_multiplier=-1))
    G.wait(t_i2c)

    def g_init3(g):
        for ch in range(NDC):
            g.memset(inds[0:64, 16 * ch + 2 * ch:16 * ch + 2 * ch + 1], 1.0)
            last = g.memset(inds[64:128, 16 * ch + 2 * ch + 1:16 * ch + 2 * ch + 2], 1.0)
        return last
    tok_consts = G.emit(g_init3)

    xsts = [scr[:, 0:1024], rxF[:, 1024:2048]]
    xn_tok = scrB[:, 2048:3072]
    xsq = scrB[:, 3072:4096]
    mark_phase("consts")
    tok_ln = tok_consts
    tok_ln_hist = [None, None]
    tok_ev = None
    tok_sqf = None
    tok_sqf_g = None
    tok_tnsq = None
    tok_nsqd = None
    for tt in range(NT):
        if tt in (1, 2, 8, 16, 24):
            mark_phase(f"p0t{tt}")
        b, t0 = tt // 16, (tt % 16) * 128
        col = b * S + t0
        xst = xsts[tt % 2]
        Q.wait(tok_ln_hist[tt % 2] if tt >= 2 else (tok_ln if tt == 0 else None))
        tok_x = Q.dma(lambda q, _b=b, _t0=t0, _x=xst:
                      q.dma_start(out=_x, in_=xin[_b, _t0:_t0 + 128, :]))
        V.wait(tok_x)

        V.seq([
            lambda v, _x=xst: v.reduce_sum(out=sumx, in_=_x, axis=AX.X),
            lambda v: v.tensor_scalar_mul(out=mean, in0=sumx, scalar1=1.0 / D),
            lambda v: v.tensor_mul(out=msq, in0=mean, in1=mean),
        ])
        if tt == 0:
            mark_phase("m_stats")
        SE.wait(tok_x)
        SE.wait(tok_sqf_g)
        SE.wait(tok_tnsq)
        tok_sq = SE.emit(lambda s, _x=xst: s.activation(out=xsq, in_=_x, func=AF.Square,
                                                        accum_out=sumsq))
        if tt == 0:
            mark_phase("m_sq")
        V.wait(tok_sq)

        tok_var = V.seq([
            lambda v: v.scalar_tensor_tensor(out=varep, in0=sumsq, scalar=1.0 / D, in1=msq,
                                             op0=ALU.mult, op1=ALU.subtract),
            lambda v: v.tensor_scalar_add(out=varep, in0=varep, scalar1=EPS),
        ])
        if tt == 0:
            mark_phase("m_var")
        SE.wait(tok_var)
        tok_sd = SE.emit(lambda s: s.activation(out=sd, in_=varep, func=AF.Sqrt))
        if tt == 0:
            mark_phase("m_sd")
        V.wait(tok_sd)

        tok_ln = V.seq([
            lambda v: v.reciprocal(out=r0n, in_=sd),
            lambda v: v.tensor_mul(out=hhn, in0=r0n, in1=r0n),
            lambda v: v.tensor_mul(out=hhn, in0=hhn, in1=varep),
            lambda v: v.tensor_scalar(out=f1n, in0=hhn, scalar1=-0.5, scalar2=1.5,
                                      op0=ALU.mult, op1=ALU.add),
            lambda v: v.tensor_mul(out=rstd, in0=r0n, in1=f1n),
            lambda v, _x=xst: v.tensor_scalar(out=rxF[:, 0:1024], in0=_x, scalar1=mean,
                                              scalar2=None, op0=ALU.subtract),
            lambda v: v.tensor_scalar(out=xn_tok, in0=rxF[:, 0:1024], scalar1=rstd,
                                      scalar2=None, op0=ALU.mult),
        ])
        tok_ln_hist[tt % 2] = tok_ln
        if tt == 0:
            mark_phase("m_ln")
        TE.wait(tok_ln)
        TE.wait(tok_ev)
        TE.wait(tok_sqf)

        def t_tr(t):
            for ch in range(NDC):
                last = t.transpose(out=PSB[0][:, ch * 128:(ch + 1) * 128],
                                   in_=xn_tok[:, ch * 128:(ch + 1) * 128],
                                   identity=idnb[:, :])
            return last
        tok_tr = TE.emit(t_tr)
        if tt == 0:
            mark_phase("m_tr")
        V.wait(tok_tr)
        def v_ev(v, _col=col):
            for ch in range(NDC):
                last = v.tensor_copy(out=xnT[:, ch * T + _col:ch * T + _col + 128],
                                     in_=PSB[0][:, ch * 128:(ch + 1) * 128])
            return last
        tok_ev = V.emit(v_ev)
        if tt == 0:
            mark_phase("m_ev")
        SE.wait(tok_tr)
        SE.wait(tok_ev)
        tok_sqf = SE.emit(lambda s: s.activation(out=xsq, in_=PSB[0][:, :], func=AF.Square))
        if tt == 0:
            mark_phase("m_sqf")
        V.wait(tok_sqf)

        tok_sqf_g = V.seq([
            lambda v: v.tensor_reduce(out=tmp1, in_=xsq, axis=AX.X, op=ALU.max),
            lambda v: v.tensor_max(out=g1acc[:, :], in0=g1acc[:, :], in1=tmp1),
        ])
        if tt == 0:
            mark_phase("m_g1")
        TE.wait(tok_sqf)
        TE.wait(tok_nsqd)

        def t_nsq(t):
            for ch in range(NDC):
                last = t.matmul(PS[1][0:16, 0:128], inds[:, 16 * ch:16 * ch + 16],
                                xsq[:, ch * 128:(ch + 1) * 128],
                                start=(ch == 0), stop=(ch == NDC - 1))
            return last
        tok_tnsq = TE.emit(t_nsq)
        if tt == 0:
            mark_phase("m_nsq")
        V.wait(tok_tnsq)

        tok_nsqd = V.seq([
            lambda v: v.tensor_reduce(out=tmp1[0:16, :], in_=PS[1][0:16, 0:128],
                                      axis=AX.X, op=ALU.max),
            lambda v, _b=b: v.tensor_max(out=mbh[0:16, _b:_b + 1],
                                         in0=mbh[0:16, _b:_b + 1], in1=tmp1[0:16, :]),
        ])
    p0_ln_last = tok_ln
    p0_tr_last = tok_tr
    p0_sqf_last = tok_sqf
    p0_sqfg_last = tok_sqf_g
    p0_tnsq_last = tok_tnsq
    p0_nsq_last = tok_nsqd
    p0_ev_last = tok_ev

    def allreduce(idx, fill_tok, post_v_fn, post_bcasts):
        Q.wait(fill_tok)
        tw = Q.dma(lambda q: q.dma_start(out=arin[idx][:, :], in_=arsb[:, :]))
        G.wait(tw)
        tcc = G.emit(lambda g: g.collective_compute(
            "AllReduce", ALU.max, replica_groups=[list(range(NCORES))],
            ins=[arin[idx][:, :].opt()], outs=[arout[idx][:, :].opt()]))
        Q.wait(tcc)
        tr = Q.dma(lambda q: q.dma_start(out=arsb[:, :], in_=arout[idx][:, :]))
        return tr

    ps2 = {"tok": None}

    def part_max(acc_ap, slot, prod_tok):
        TE.wait(prod_tok)
        TE.wait(ps2["tok"])
        tok_t = TE.emit(lambda t, _a=acc_ap: t.transpose(
            out=PSB[2][0:1, 0:128], in_=_a, identity=idnb[:, :]))
        V.wait(tok_t)
        tok_v = V.emit(lambda v, _s=slot: v.tensor_reduce(
            out=arsb[0:1, _s:_s + 1], in_=PSB[2][0:1, 0:128], axis=AX.X, op=ALU.max))
        ps2["tok"] = tok_v
        return tok_v

    def bcast(dst_cols, src_lo, width, src_prod_tok):
        V.wait(src_prod_tok)

        tok_sp = V.seq([
            lambda v: v.tensor_copy(out=smB[0:1, 0:width], in_=sm1[0:1, src_lo:src_lo + width]),
            lambda v: v.tensor_sub(out=sm1[0:1, 56:56 + width],
                                   in0=sm1[0:1, src_lo:src_lo + width], in1=smB[0:1, 0:width]),
            lambda v: v.tensor_copy(out=smB[0:1, 32:32 + width], in_=sm1[0:1, 56:56 + width]),
        ])
        TE.wait(tok_sp)
        TE.wait(ps2["tok"])

        def t_bc(t):
            t.matmul(PS[2][:, 0:width], tri[0:1, :], smB[0:1, 0:width],
                     start=True, stop=False)
            return t.matmul(PS[2][:, 0:width], tri[0:1, :], smB[0:1, 32:32 + width],
                            start=False, stop=True)
        tok_bc = TE.emit(t_bc)
        V.wait(tok_bc)
        tok_cp = V.emit(lambda v: v.tensor_copy(out=scl[:, dst_cols], in_=PS[2][:, 0:width]))
        ps2["tok"] = tok_cp
        return tok_cp

    mark_phase("p0")
    tok_c1 = part_max(g1acc[:, :], 0, p0_sqfg_last)
    tok_r1 = allreduce(0, tok_c1, None, None)
    SE.wait(tok_r1)
    tok_g1s = SE.emit(lambda s: s.activation(out=sm1[0:1, 2:3], in_=arsb[0:1, 0:1],
                                             func=AF.Sqrt))
    V.wait(tok_g1s)

    tok_s1 = V.seq([
        lambda v: v.tensor_scalar(out=sm1[0:1, 0:1], in0=sm1[0:1, 2:3], scalar1=1.0 / QMAX,
                                  scalar2=1e-12, op0=ALU.mult, op1=ALU.max),
        lambda v: v.reciprocal(out=sm1[0:1, 1:2], in_=sm1[0:1, 0:1]),
    ])
    tok_bc1 = bcast(slice(SCL_RS1, SCL_RS1 + 1), 1, 1, tok_s1)
    TE.wait(p0_nsq_last)
    TE.wait(ps2["tok"])

    def t_mbh(t):
        t.transpose(out=PSB[2][0:1, 0:16], in_=mbh[0:16, 0:1], identity=idnb[0:16, 0:16])
        return t.transpose(out=PSB[2][0:1, 16:32], in_=mbh[0:16, 1:2],
                           identity=idnb[0:16, 0:16])
    tok_mt = TE.emit(t_mbh)
    V.wait(tok_mt)
    tok_mcp = V.emit(lambda v: v.tensor_copy(out=smB[0:1, 0:32], in_=PSB[2][0:1, 0:32]))
    TE.wait(tok_mcp)
    tok_mbc = TE.emit(lambda t: t.matmul(PS[2][:, 0:32], tri[0:1, :], smB[0:1, 0:32],
                                         start=True, stop=True))
    V.wait(tok_mbc)
    tok_bias = V.emit(lambda v: v.tensor_scalar_mul(out=bias_all[:, :],
                                                    in0=PS[2][:, 0:32], scalar1=-0.125))
    ps2["tok"] = tok_bias

    mark_phase("ar1")
    tok_vvb = None
    for tt in range(NT):
        b, t0 = tt // 16, (tt % 16) * 128
        col = b * S + t0
        if tt == 0:
            TE.wait(p0_ev_last)
            TE.wait(p0_sqf_last)
        TE.wait(tok_vvb)

        def t_vb(t, _col=col):
            for ch in range(NDC):
                last = t.transpose(out=PSB[0][:, ch * 128:(ch + 1) * 128],
                                   in_=xnT3[:, ch, _col:_col + 128],
                                   identity=idnb[:, :])
            return last
        tok_tvb = TE.emit(t_vb)
        V.wait(tok_tvb)
        tok_vvb = V.emit(lambda v, _tt=tt: v.tensor_copy(
            out=vbuf[:, _tt * 1040:(_tt + 1) * 1040].rearrange("p (h c) -> p h c", c=65)[:, :, 0:64],
            in_=PSB[0][:, :].rearrange("p (h c) -> p h c", c=64)))
    vbuf_done = tok_vvb

    mark_phase("p1a")
    rxt = scr[:, 1024:1536]
    rxi = scrI[:, 1536:2048]
    rxc = scr[:, 0:512]
    tok_rx = None
    for k in range(NDC * NTK):
        cols = slice(k * 512, (k + 1) * 512)
        SE.wait(tok_rx)
        if k == 0:
            SE.wait(tok_bc1)
            SE.wait(p0_tr_last)
            SE.wait(p0_tnsq_last)
            SE.wait(p0_sqfg_last)
        tok_pr = SE.emit(lambda s, _c=cols: s.activation(
            out=rxt, in_=xnT[:, _c], func=AF.Copy,
            scale=scl[:, SCL_RS1:SCL_RS1 + 1], bias=RSH))
        V.wait(tok_pr)

        tok_rx = V.seq([
            lambda v: v.tensor_copy(out=rxi, in_=rxt),
            lambda v: v.scalar_tensor_tensor(out=rxc, in0=rxt, scalar=-0.5, in1=rxi,
                                             op0=ALU.add, op1=ALU.is_ge),
            lambda v, _c=cols: v.scalar_tensor_tensor(out=rxb[:, _c], in0=rxi, scalar=-RSH,
                                                      in1=rxc, op0=ALU.add, op1=ALU.add),
        ])
    rx_all = tok_rx

    mark_phase("p1b")
    wraw = scr[:, 0:1024]
    wstg = scr[:, 1024:1536]
    spare_bf = scrB[:, 3072:3584]
    ust = [scrB[:, 3584 + 512 * p:3584 + 512 * p + 512] for p in range(2)]

    def emit_dequant(Vq, wraw_ap, wraw_apI, wstg_ap, spare_ap, wdq_halves):
        tok_half = None
        for h in range(2):
            hw3 = wraw_ap[:, h * 512:(h + 1) * 512].rearrange("p (a c) -> p a c", c=32)
            iv = wraw_apI[:, 512 * h:512 * h + 512]
            wdq = wdq_halves[h]

            tok_half = Vq.seq([
                lambda v, _hw3=hw3: v.tensor_reduce(
                    out=wsc[:, :].rearrange("p (a c) -> p a c", c=1), in_=_hw3,
                    axis=AX.X, op=ALU.max, apply_absolute_value=True),
                lambda v: v.tensor_scalar(out=wss[:, :], in0=wsc[:, :], scalar1=1.0 / 7.0,
                                          scalar2=1e-12, op0=ALU.mult, op1=ALU.max),
                lambda v: v.reciprocal(out=wsc[:, :], in_=wss[:, :]),
                lambda v, _hw3=hw3, _ws=wstg_ap: v.tensor_tensor(
                    out=_ws.rearrange("p (a c) -> p a c", c=32), in0=_hw3,
                    in1=wsc[:, :].rearrange("p (a c) -> p a c", c=1).broadcast_to([128, 16, 32]),
                    op=ALU.mult),
                lambda v, _ws=wstg_ap: v.tensor_scalar_add(out=_ws, in0=_ws, scalar1=8.0),
                lambda v, _iv=iv, _ws=wstg_ap: v.tensor_copy(out=_iv, in_=_ws),
                lambda v, _iv=iv, _ws=wstg_ap: v.scalar_tensor_tensor(
                    out=_ws, in0=_ws, scalar=-0.5, in1=_iv, op0=ALU.add, op1=ALU.is_ge),
                lambda v, _iv=iv, _ws=wstg_ap, _sp=spare_ap: v.scalar_tensor_tensor(
                    out=_sp, in0=_iv, scalar=-8.0, in1=_ws, op0=ALU.add, op1=ALU.add),
                lambda v, _wdq=wdq, _sp=spare_ap: v.tensor_tensor(
                    out=_wdq.rearrange("p (a c) -> p a c", c=32),
                    in0=_sp.rearrange("p (a c) -> p a c", c=32),
                    in1=wss[:, :].rearrange("p (a c) -> p a c", c=1).broadcast_to([128, 16, 32]),
                    op=ALU.mult),
            ])
        return tok_half

    wdq_h_p1 = [scrB[:, 0:512], scrB[:, 1024:1536]]
    seq = [(w, ic) for w in ["wu", "wg"] for ic in range(NIC)]
    evac_tok = [None, None]
    stage_free = [None, None]
    wt_free = [None, None]
    wraw_ok = rx_all
    tok_mm = None
    tok_vm = None
    tok_spill = None
    tok_wtev_prev = None
    first_mm = True
    for n, (w, ic) in enumerate(seq):
        par = n % 2
        Q.wait(wraw_ok)
        tok_w = Q.dma(lambda q, _w=w, _ic=ic: q.dma_start(
            out=wraw, in_=w_in[_w][_ic * 128:(_ic + 1) * 128, :]))
        V.wait(tok_w)
        if n == 0:
            V.wait(p0_tr_last)
        tok_half = emit_dequant(V, wraw, scrI, wstg, spare_bf, wdq_h_p1)
        wraw_ok = tok_half
        TE.wait(tok_half)
        TE.wait(wt_free[par])
        TE.wait(tok_wtev_prev if tok_wtev_prev is not None else ps2["tok"])

        def t_wt(t):
            for h in range(2):
                for k2 in range(4):
                    last = t.transpose(
                        out=PSB[2][:, (4 * h + k2) * 128:(4 * h + k2 + 1) * 128],
                        in_=scrB[:, 1024 * h + k2 * 128:1024 * h + (k2 + 1) * 128],
                        identity=idnb[:, :])
            return last
        tok_twt = TE.emit(t_wt)
        V.wait(tok_twt)
        tok_wtev = V.emit(lambda v, _par=par: v.tensor_copy(
            out=wt[:, _par * 1024:(_par + 1) * 1024], in_=PSB[2][:, :]))
        tok_wtev_prev = tok_wtev
        mxt = mx["U"] if w == "wu" else mx["G"]
        TE.wait(tok_wtev)
        for tk in range(NTK):
            it = n * NTK + tk
            pp = it % 2
            if first_mm:
                TE.wait(vbuf_done)
                TE.wait(p0_nsq_last)
                first_mm = False
            TE.wait(evac_tok[pp])

            def t_mm1(t, _par=par, _tk=tk, _pp=pp):
                for dc in range(NDC):
                    last = t.matmul(PS[_pp][:, :],
                                    wt[:, _par * 1024 + dc * 128:_par * 1024 + (dc + 1) * 128],
                                    rxb[:, dc * T + _tk * 512:dc * T + (_tk + 1) * 512],
                                    start=(dc == 0), stop=(dc == NDC - 1))
                return last
            tok_mm = TE.emit(t_mm1)
            up = it % 2
            SE.wait(tok_mm)
            SE.wait(stage_free[up])
            tok_evs = SE.emit(lambda s, _pp=pp, _up=up: s.activation(
                out=ust[_up], in_=PS[_pp][:, :], func=AF.Copy))
            evac_tok[pp] = tok_evs
            V.wait(tok_evs)

            tok_vm = V.seq([
                lambda v, _up=up: v.tensor_reduce(out=tmp1, in_=ust[_up], axis=AX.X,
                                                  op=ALU.max, apply_absolute_value=True),
                lambda v, _mx=mxt: v.tensor_max(out=_mx[:, :], in0=_mx[:, :], in1=tmp1),
            ])
            SE.wait(tok_vm)
            tok_spill = SE.dma(lambda q, _sl=(0 if w == "wu" else 1), _ic=ic, _tk=tk, _up=up:
                               q.dma_start(out=spUG[_ic, _tk, :, _sl, :], in_=ust[_up]))
            stage_free[up] = tok_spill
        wt_free[par] = tok_mm
    upgate_t_done = tok_mm
    upgate_v_done = tok_vm
    upgate_spill_done = tok_spill

    mark_phase("p1c")
    wdraw = rxF[:, 12288:13312]
    wdrawI = rxI[:, 12288:13312]
    wdstg = rxF[:, 13312:13824]
    wd_spare = rxb[:, 27648:28160]
    wdq_h_wd = [rxb[:, 28672:29184], rxb[:, 29184:29696]]
    wd_prev_dma = wraw_ok
    tok_wdev = None
    for dc in range(NDC):
        for sc3 in range(3):
            Q.wait(wd_prev_dma)
            if dc == 0 and sc3 == 0:
                Q.wait(upgate_t_done)
            tok_w = Q.dma(lambda q, _dc=dc, _s=sc3: q.dma_start(
                out=wdraw, in_=wd_in[_dc * 128:(_dc + 1) * 128, _s * 1024:(_s + 1) * 1024]))
            V.wait(tok_w)
            if dc == 0 and sc3 == 0:
                V.wait(upgate_t_done)
            tok_half = emit_dequant(V, wdraw, wdrawI, wdstg, wd_spare, wdq_h_wd)
            wd_prev_dma = tok_half
            TE.wait(tok_half)
            TE.wait(tok_wdev if tok_wdev is not None else tok_wtev_prev)

            def t_wd(t):
                for h in range(2):
                    for k2 in range(4):
                        last = t.transpose(
                            out=PSB[2][:, (4 * h + k2) * 128:(4 * h + k2 + 1) * 128],
                            in_=wdq_h_wd[h][:, k2 * 128:(k2 + 1) * 128],
                            identity=idnb[:, :])
                return last
            tok_twd = TE.emit(t_wd)
            V.wait(tok_twd)
            ic0 = sc3 * 8

            def v_wdev(v, _ic0=ic0, _dc=dc):
                return v.tensor_copy(
                    out=rxb[:, :].rearrange("p (ic x) -> p ic x", x=1024)[:, _ic0:_ic0 + 8,
                                                                          _dc * 128:(_dc + 1) * 128],
                    in_=PSB[2][:, :].rearrange("p (a c) -> p a c", c=128))
            tok_wdev = V.emit(v_wdev)
    wd_ready = tok_wdev

    mark_phase("wd")
    ps2["tok"] = wd_ready if ps2["tok"] is None else ps2["tok"]
    TE.wait(wd_ready)
    part_max(mx["U"][:, :], 1, upgate_v_done)
    tok_c2b = part_max(mx["G"][:, :], 2, upgate_v_done)
    tok_r2 = allreduce(1, tok_c2b, None, None)
    V.wait(tok_r2)

    tok_s23 = V.seq([
        lambda v: v.scalar_tensor_tensor(out=sm1[0:1, 4:5], in0=arsb[0:1, 1:2], scalar=1.0,
                                         in1=sm1[0:1, 0:1], op0=ALU.bypass, op1=ALU.mult),
        lambda v: v.tensor_scalar(out=sm1[0:1, 4:5], in0=sm1[0:1, 4:5], scalar1=1.0 / QMAX,
                                  scalar2=1e-12, op0=ALU.mult, op1=ALU.max),
        lambda v: v.reciprocal(out=sm1[0:1, 5:6], in_=sm1[0:1, 4:5]),
        lambda v: v.scalar_tensor_tensor(out=sm1[0:1, 7:8], in0=arsb[0:1, 2:3], scalar=1.0,
                                         in1=sm1[0:1, 0:1], op0=ALU.bypass, op1=ALU.mult),
        lambda v: v.tensor_scalar(out=sm1[0:1, 7:8], in0=sm1[0:1, 7:8], scalar1=1.0 / QMAX,
                                  scalar2=1e-12, op0=ALU.mult, op1=ALU.max),
        lambda v: v.reciprocal(out=sm1[0:1, 41:42], in_=sm1[0:1, 7:8]),
        lambda v: v.scalar_tensor_tensor(out=sm1[0:1, 51:52], in0=sm1[0:1, 5:6], scalar=1.0,
                                         in1=sm1[0:1, 0:1], op0=ALU.bypass, op1=ALU.mult),
        lambda v: v.scalar_tensor_tensor(out=sm1[0:1, 52:53], in0=sm1[0:1, 41:42], scalar=1.0,
                                         in1=sm1[0:1, 0:1], op0=ALU.bypass, op1=ALU.mult),
        lambda v: v.tensor_copy(out=sm1[0:1, 53:54], in_=sm1[0:1, 7:8]),
    ])
    scales2_tok = bcast(slice(SCL_AU, SCL_S3 + 1), 51, 3, tok_s23)

    mark_phase("ar2")
    slab = [[scrB[:, (2 * e + par) * 512:(2 * e + par) * 512 + 512] for par in range(2)]
            for e in range(2)]
    sstg = [scrB[:, 2048 + 512 * e:2048 + 512 * e + 512] for e in range(2)]
    sstg2 = [scrB[:, 3072:3328] for e in range(2)]
    SCB = [[0, 1], [3, 4]]
    PVB = [5, 6]
    P2B = 7

    rbs = [rxb[:, 24576 + 512 * k:24576 + 512 * (k + 1)] for k in range(4)]
    p2t = rxF[:, 13312:13824]
    p2tB = scr[:, 1792:2304]
    p2i = rxI[:, 13824:14336]
    p2c = rxF[:, 14336:14848]
    rgs = rxb[:, 29696:30208]
    rss = rxb[:, 30208:30720]
    m1s = rxb[:, 30720:31232]
    rup = rxb[:, 31232:31744]
    sgs = rxb[:, 31744:32256]

    att = {"slab_free": {}, "sc_free": {}, "pv_free": [None, None],
           "p2_free": None, "sstg_free": [None, None], "sstg2_free": [None, None]}
    first_exp = True
    first_sc = True

    def emit_attention_iter(b, hp, j):
        nonlocal first_exp, first_sc
        C = 4 * j + 4
        base = hp * T + b * S
        exp_toks = {}
        pv_last = {}
        for c in range(C):
            off = max(0, 128 * (c - 4 * j))
            N = 512 - off
            mm_toks = {}
            for e in range(2):
                bank = SCB[e][c % 2]
                if first_sc:
                    TE.wait(evac_tok[0])
                    TE.wait(evac_tok[1])
                    first_sc = False
                TE.wait(att["sc_free"].get((e, c % 2)))

                def t_sc(t, _e=e, _bank=bank, _c=c, _N=N, _base=base, _j=j, _off=off):
                    return t.matmul(PS[_bank][:, 0:_N],
                                    xnT[64 * _e:64 * _e + 64,
                                        _base + _c * 128:_base + (_c + 1) * 128],
                                    xnT[64 * _e:64 * _e + 64,
                                        _base + 512 * _j + _off:_base + 512 * (_j + 1)],
                                    start=True, stop=True, tile_position=(64 * _e, 0))
                mm_toks[e] = TE.emit(t_sc)
            for e in range(2):
                SE.wait(mm_toks[e])
                SE.wait(att["slab_free"].get((e, c % 2)))
                if first_exp:
                    SE.wait(tok_bias)
                    first_exp = False
                hidx = b * 16 + 2 * hp + e
                bank = SCB[e][c % 2]
                tok_e = SE.emit(lambda s, _e=e, _c=c, _N=N, _bk=bank, _h=hidx:
                                s.activation(out=slab[_e][_c % 2][:, 0:_N],
                                             in_=PS[_bk][:, 0:_N], func=AF.Exp,
                                             scale=0.125, bias=bias_all[:, _h:_h + 1]))
                att["sc_free"][(e, c % 2)] = tok_e
                if c >= 4 * j:
                    V.wait(tok_e)
                    tok_e = V.emit(lambda v, _e=e, _c=c: v.tensor_mul(
                        out=slab[_e][_c % 2][:, 0:128],
                        in0=slab[_e][_c % 2][:, 0:128], in1=tri[:, :]))
                exp_toks[(e, c)] = tok_e
            for e in range(2):
                tile = b * 16 + c
                TE.wait(exp_toks[(e, c)])
                if c == 0:
                    TE.wait(att["pv_free"][e])

                def t_pv(t, _e=e, _c=c, _off=off, _N=N, _tile=tile, _hp=hp, _C=C):
                    return t.matmul(PS[PVB[_e]][0:65, _off:512],
                                    vbuf[:, _tile * 1040 + (2 * _hp + _e) * 65:
                                         _tile * 1040 + (2 * _hp + _e) * 65 + 65],
                                    slab[_e][_c % 2][:, 0:_N],
                                    start=(_c == 0), stop=(_c == _C - 1))
                tok_pv = TE.emit(t_pv)
                att["slab_free"][(e, c % 2)] = tok_pv
                pv_last[e] = tok_pv
        for e in range(2):
            V.wait(pv_last[e])
            V.wait(att["sstg_free"][e])
            tok_cp = V.emit(lambda v, _e=e: v.tensor_copy(out=sstg[_e][0:65, :],
                                                          in_=PS[PVB[_e]][0:65, :]))
            att["pv_free"][e] = tok_cp
            TE.wait(tok_cp)
            TE.wait(att["p2_free"])

            def t_n(t, _e=e):
                for blk in range(4):
                    last = t.transpose(
                        out=PSB[P2B][:, _e * 512 + blk * 66:_e * 512 + blk * 66 + 65],
                        in_=sstg[_e][0:65, blk * 128:(blk + 1) * 128],
                        identity=idnb[0:65, 0:65])
                return last
            tok_tn = TE.emit(t_n)
            att["sstg_free"][e] = tok_tn
            V.wait(tok_tn)
            V.wait(att["sstg2_free"][0])
            V.wait(att["sstg2_free"][1])

            tok_vn = V.seq([
                lambda v, _e=e: v.reciprocal(
                    out=rd4.rearrange("p (a c) -> p a c", c=1),
                    in_=PSB[P2B][:, _e * 512 + 64:_e * 512 + 64 + 4 * 66]
                    .rearrange("p (a c) -> p a c", c=66)[:, :, 0:1]),
                lambda v, _e=e: v.tensor_tensor(
                    out=sstg2[_e].rearrange("p (a c) -> p a c", c=64),
                    in0=PSB[P2B][:, _e * 512:_e * 512 + 264]
                    .rearrange("p (a c) -> p a c", c=66)[:, :, 0:64],
                    in1=rd4.rearrange("p (a c) -> p a c", c=1).broadcast_to([128, 4, 64]),
                    op=ALU.mult),
            ])
            att["p2_free"] = tok_vn
            h = 2 * hp + e
            Q.wait(tok_vn)
            last_d = Q.dma(lambda q, _e=e, _h=h, _ti=b * 16 + 4 * j:
                           q.dma_start(
                               out=sout_d[_ti:_ti + 4, :, _h * 64:(_h + 1) * 64]
                               .rearrange("a p c -> p a c"),
                               in_=sstg2[_e].rearrange("p (a c) -> p a c", c=64)))
            att["sstg2_free"][e] = last_d

    p1_iters = [(ic, tk) for ic in range(NIC) for tk in range(NTK)]
    p1 = {"rb": [None, None], "last_v": None, "spill": None}

    def emit_pass1_iter(idx):
        ic, tk = p1_iters[idx]
        sl = idx % 2
        rbU, rbG = rbs[2 * sl], rbs[2 * sl + 1]
        rbPair = rxb[:, 24576 + 1024 * sl:24576 + 1024 * (sl + 1)]
        Q.wait(p1["rb"][sl])
        if idx == 0:
            Q.wait(upgate_spill_done)
            Q.wait(wd_ready)
        d2 = Q.dma(lambda q, _ic=ic, _tk=tk, _r=rbPair: q.dma_start(
            out=_r.rearrange("p (a c) -> p a c", c=512),
            in_=spUG[_ic, _tk, :, :, :]))
        SE.wait(d2)
        if idx == 0:
            SE.wait(scales2_tok)
        SE.wait(p1["last_v"])
        tU = SE.emit(lambda s, _r=rbU: s.activation(out=p2t, in_=_r, func=AF.Copy,
                                                    scale=scl[:, SCL_AU:SCL_AU + 1], bias=RSH))
        V.wait(tU)

        v1 = V.seq([
            lambda v: v.tensor_copy(out=p2i, in_=p2t),
            lambda v: v.scalar_tensor_tensor(out=p2c, in0=p2t, scalar=-0.5, in1=p2i,
                                             op0=ALU.add, op1=ALU.is_ge),
            lambda v: v.scalar_tensor_tensor(out=rup, in0=p2i, scalar=-RSH, in1=p2c,
                                             op0=ALU.add, op1=ALU.add),
        ])
        SE.wait(v1)
        SE.dma(lambda q, _ic=ic, _tk=tk: q.dma_start(out=spMR[_ic, _tk, :, 1, :], in_=rup))
        SE.wait(v1)
        tG = SE.emit(lambda s, _r=rbG: s.activation(out=p2tB, in_=_r, func=AF.Copy,
                                                    scale=scl[:, SCL_AG:SCL_AG + 1], bias=RSH))
        V.wait(tG)
        g1t = V.seq([
            lambda v: v.tensor_copy(out=p2i, in_=p2tB),
            lambda v: v.scalar_tensor_tensor(out=p2c, in0=p2tB, scalar=-0.5, in1=p2i,
                                             op0=ALU.add, op1=ALU.is_ge),
            lambda v: v.scalar_tensor_tensor(out=rgs, in0=p2i, scalar=-RSH, in1=p2c,
                                             op0=ALU.add, op1=ALU.add),
        ])
        SE.wait(g1t)
        tSg = SE.emit(lambda s: s.activation(out=sgs, in_=rgs, func=AF.Sigmoid,
                                             scale=scl[:, SCL_S3:SCL_S3 + 1]))
        V.wait(tSg)
        v2 = V.seq([
            lambda v: v.tensor_scalar_mul(out=p2tB, in0=sgs, scalar1=65536.0),
            lambda v: v.tensor_copy(out=p2i, in_=p2tB),
            lambda v: v.scalar_tensor_tensor(out=p2c, in0=p2tB, scalar=-0.5, in1=p2i,
                                             op0=ALU.add, op1=ALU.is_ge),
            lambda v: v.scalar_tensor_tensor(out=p2c, in0=p2i, scalar=0.0, in1=p2c,
                                             op0=ALU.add, op1=ALU.add),
            lambda v: v.tensor_scalar_min(out=rss, in0=p2c, scalar1=65535.0),
            lambda v: v.tensor_mul(out=m1s, in0=rgs, in1=rss),
            lambda v: v.tensor_reduce(out=tmp1, in_=m1s, axis=AX.X, op=ALU.max,
                                      apply_absolute_value=True),
            lambda v: v.tensor_max(out=mx["M1"][:, :], in0=mx["M1"][:, :], in1=tmp1),
        ])
        SE.wait(v2)
        dm1 = SE.dma(lambda q, _ic=ic, _tk=tk: q.dma_start(out=spMR[_ic, _tk, :, 0, :], in_=m1s))
        p1["rb"][sl] = tSg
        p1["last_v"] = v2
        p1["spill"] = dm1

    p2st = {"rb": [None, None], "last_v": None, "spill": None}

    def emit_pass2_iter(idx, gate_tok, p1_spill_tok):
        ic, tk = p1_iters[idx]
        sl = idx % 2
        rbM, rbR = rbs[2 * sl], rbs[2 * sl + 1]
        rbPair = rxb[:, 24576 + 1024 * sl:24576 + 1024 * (sl + 1)]
        Q.wait(p2st["rb"][sl])
        if idx == 0:
            Q.wait(p1_spill_tok)
        d2 = Q.dma(lambda q, _ic=ic, _tk=tk, _r=rbPair: q.dma_start(
            out=_r.rearrange("p (a c) -> p a c", c=512),
            in_=spMR[_ic, _tk, :, :, :]))
        SE.wait(d2)
        if idx == 0:
            SE.wait(gate_tok)
        SE.wait(p2st["last_v"])
        tM = SE.emit(lambda s, _r=rbM: s.activation(out=p2t, in_=_r, func=AF.Copy,
                                                    scale=scl[:, SCL_AM:SCL_AM + 1], bias=RSH))
        V.wait(tM)

        v2 = V.seq([
            lambda v: v.tensor_copy(out=p2i, in_=p2t),
            lambda v: v.scalar_tensor_tensor(out=p2c, in0=p2t, scalar=-0.5, in1=p2i,
                                             op0=ALU.add, op1=ALU.is_ge),
            lambda v: v.scalar_tensor_tensor(out=rgs, in0=p2i, scalar=-RSH, in1=p2c,
                                             op0=ALU.add, op1=ALU.add),
            lambda v, _r=rbR: v.tensor_mul(out=m1s, in0=rgs, in1=_r),
            lambda v: v.tensor_reduce(out=tmp1, in_=m1s, axis=AX.X, op=ALU.max,
                                      apply_absolute_value=True),
            lambda v: v.tensor_max(out=mx["Uu"][:, :], in0=mx["Uu"][:, :], in1=tmp1),
        ])
        SE.wait(v2)
        du = SE.dma(lambda q, _ic=ic, _tk=tk: q.dma_start(out=spUu[_ic, _tk, :, :], in_=m1s))
        p2st["rb"][sl] = v2
        p2st["last_v"] = v2
        p2st["spill"] = du

    ar3_gate = None
    p1_idx = 0
    p2_idx = 0
    p1_spill_last = None
    atts = [(b, hp) for b in range(2) for hp in range(8)]
    for ai, (b, hp) in enumerate(atts):
        for j in range(4):
            emit_attention_iter(b, hp, j)
        if ai < 10:
            for _ in range(min(20, 192 - p1_idx)):
                emit_pass1_iter(p1_idx)
                p1_idx += 1
        if ai == 9:
            assert p1_idx == 192
            p1_spill_last = p1["spill"]
            tok_c3 = part_max(mx["M1"][:, :], 3, p1["last_v"])
            tok_r3 = allreduce(2, tok_c3, None, None)
            V.wait(tok_r3)

            tok_s4 = V.seq([
                lambda v: v.scalar_tensor_tensor(out=sm1[0:1, 42:43], in0=arsb[0:1, 3:4],
                                                 scalar=1.0 / 65536.0, in1=sm1[0:1, 7:8],
                                                 op0=ALU.mult, op1=ALU.mult),
                lambda v: v.tensor_scalar(out=sm1[0:1, 43:44], in0=sm1[0:1, 42:43],
                                          scalar1=1.0 / QMAX, scalar2=1e-12,
                                          op0=ALU.mult, op1=ALU.max),
                lambda v: v.reciprocal(out=sm1[0:1, 44:45], in_=sm1[0:1, 43:44]),
                lambda v: v.scalar_tensor_tensor(out=sm1[0:1, 54:55], in0=sm1[0:1, 44:45],
                                                 scalar=1.0 / 65536.0, in1=sm1[0:1, 7:8],
                                                 op0=ALU.mult, op1=ALU.mult),
            ])
            ar3_gate = bcast(slice(SCL_AM, SCL_AM + 1), 54, 1, tok_s4)
        if ai >= 10:
            for _ in range(min(32, 192 - p2_idx)):
                emit_pass2_iter(p2_idx, ar3_gate, p1_spill_last)
                p2_idx += 1
    assert p2_idx == 192

    mark_phase("p2")
    tok_c4 = part_max(mx["Uu"][:, :], 4, p2st["last_v"])
    tok_r4 = allreduce(3, tok_c4, None, None)
    V.wait(tok_r4)

    tok_s5 = V.seq([
        lambda v: v.scalar_tensor_tensor(out=sm1[0:1, 46:47], in0=sm1[0:1, 43:44], scalar=1.0,
                                         in1=sm1[0:1, 4:5], op0=ALU.bypass, op1=ALU.mult),
        lambda v: v.scalar_tensor_tensor(out=sm1[0:1, 47:48], in0=arsb[0:1, 4:5], scalar=1.0,
                                         in1=sm1[0:1, 46:47], op0=ALU.bypass, op1=ALU.mult),
        lambda v: v.tensor_scalar(out=sm1[0:1, 48:49], in0=sm1[0:1, 47:48], scalar1=1.0 / QMAX,
                                  scalar2=1e-12, op0=ALU.mult, op1=ALU.max),
        lambda v: v.reciprocal(out=sm1[0:1, 49:50], in_=sm1[0:1, 48:49]),
        lambda v: v.scalar_tensor_tensor(out=sm1[0:1, 55:56], in0=sm1[0:1, 49:50], scalar=1.0,
                                         in1=sm1[0:1, 46:47], op0=ALU.bypass, op1=ALU.mult),
        lambda v: v.tensor_copy(out=sm1[0:1, 56:57], in_=sm1[0:1, 48:49]),
    ])
    scales5_tok = bcast(slice(SCL_AO, SCL_S5 + 1), 55, 2, tok_s5)

    mark_phase("ar4")
    xrd = vbF[:, 0:1024]
    ostg = vbF[:, 1024:2048]
    sback = vbuf[:, 4096:5120]
    gall = [vbuf[:, 5120 + 512 * dc:5120 + 512 * (dc + 1)] for dc in range(NDC)]
    ro = [rbs[1], rbs[3]]

    dn = {"rb": [None, None], "ro_free": [None, None], "last_v": None}
    prev_ps_users = [att["sc_free"].get((0, 0)), att["sc_free"].get((0, 1)), scales5_tok,
                     att["sc_free"].get((1, 0)), att["sc_free"].get((1, 1)),
                     att["pv_free"][0], att["pv_free"][1], att["p2_free"]]
    tok_dn_evac_prev = [None] * NDC
    tok_comb_prev = None
    tok_tg_prev = None
    tok_out_last = None
    tok_cb_prev = None
    p3_first = True
    tok_dn = None
    for tk in range(NTK):
        for ic in range(NIC):
            sl = ic % 2
            Q.wait(dn["rb"][sl])
            if p3_first:
                Q.wait(p2st["spill"])
            d1 = Q.dma(lambda q, _ic=ic, _tk=tk, _r=rbs[2 * sl]:
                       q.dma_start(out=_r, in_=spUu[_ic, _tk, :, :]))
            SE.wait(d1)
            if p3_first:
                SE.wait(scales5_tok)
            SE.wait(dn["last_v"])
            tP = SE.emit(lambda s, _r=rbs[2 * sl]: s.activation(
                out=p2t, in_=_r, func=AF.Copy, scale=scl[:, SCL_AO:SCL_AO + 1], bias=RSH))
            dn["rb"][sl] = tP
            V.wait(tP)
            V.wait(dn["ro_free"][sl])

            vro = V.seq([
                lambda v: v.tensor_copy(out=p2i, in_=p2t),
                lambda v: v.scalar_tensor_tensor(out=p2c, in0=p2t, scalar=-0.5, in1=p2i,
                                                 op0=ALU.add, op1=ALU.is_ge),
                lambda v, _sl=sl: v.scalar_tensor_tensor(out=ro[_sl], in0=p2i, scalar=-RSH,
                                                         in1=p2c, op0=ALU.add, op1=ALU.add),
            ])
            dn["last_v"] = vro
            p3_first = False
            TE.wait(vro)
            if ic == 0:
                for dc in range(NDC):
                    TE.wait(prev_ps_users[dc] if tk == 0 else tok_dn_evac_prev[dc])
                if tk > 0:
                    TE.wait(tok_cb_prev)

            def t_dn(t, _sl=sl, _ic=ic):
                for dc in range(NDC):
                    last = t.matmul(PS[dc][:, :],
                                    rxb[:, _ic * 1024 + dc * 128:_ic * 1024 + (dc + 1) * 128],
                                    ro[_sl], start=(_ic == 0), stop=(_ic == NIC - 1))
                return last
            tok_dn = TE.emit(t_dn)
            dn["ro_free"][sl] = tok_dn
        tok_si = None
        for dc in range(NDC):
            SE.wait(tok_dn if dc == 0 else None)
            SE.wait(tok_tg_prev if dc == 0 else None)
            if tk == 0 and dc == 0:
                SE.wait(vbuf_done)
            tok_si = SE.emit(lambda s, _dc=dc: s.activation(
                out=gall[_dc], in_=PS[_dc][:, :], func=AF.Silu, scale=scl[:, SCL_S5:SCL_S5 + 1]))
            tok_dn_evac_prev[dc] = tok_si
        for tl in range(4):
            tt = tk * 4 + tl
            b, t0 = tt // 16, (tt % 16) * 128
            Q.wait(tok_comb_prev)
            if tt == 0:
                Q.wait(att["sstg2_free"][0])
                Q.wait(att["sstg2_free"][1])
                Q.wait(tok_dn)
            dx = Q.dma(lambda q, _b=b, _t0=t0: q.dma_start(out=xrd, in_=xin[_b, _t0:_t0 + 128, :]))
            ds = Q.dma(lambda q, _tt=tt: q.dma_start(out=sback, in_=sout_d[_tt, :, :]))
            TE.wait(tok_si)
            TE.wait(tok_comb_prev)

            def t_g(t, _tl=tl):
                for dc in range(NDC):
                    last = t.transpose(out=PSB[0][:, dc * 128:(dc + 1) * 128],
                                       in_=gall[dc][:, _tl * 128:(_tl + 1) * 128],
                                       identity=idnb[:, :])
                return last
            tok_tg = TE.emit(t_g)
            tok_tg_prev = tok_tg
            V.wait(tok_tg)
            V.wait(ds)
            V.wait(dx)

            tok_cb = V.seq([
                lambda v: v.scalar_tensor_tensor(out=ostg, in0=sback, scalar=1.0,
                                                 in1=PSB[0][:, :], op0=ALU.bypass, op1=ALU.mult),
                lambda v: v.tensor_add(out=ostg, in0=ostg, in1=xrd),
            ])
            tok_cb_prev = tok_cb
            Q.wait(tok_cb)
            tok_out_last = Q.dma(lambda q, _b=b, _t0=t0: q.dma_start(
                out=out_d[_b, _t0:_t0 + 128, :], in_=ostg))
            tok_comb_prev = tok_out_last

    if cut is not None:
        snap = PH[cut]
        for k, e in engs.items():
            e.ops = e.ops[:snap[k][0]]
            e.n = snap[k][1]
    if dbg_out:
        pairs = {"xnT": xnT[:, :], "vbuf": vbuf[:, :], "rx": rxb[:, :],
                 "bias": bias_all[:, :], "scales": scl[:, :],
                 "wdt": rxb[:, 0:NIC * 1024], "sm1": sm1[:, :]}
        Q.wait(("v", V.n))
        Q.wait(("s", SE.n))
        Q.wait(("g", G.n))
        Q.wait(("t", TE.n))
        for nm in dbg_out:
            Q.dma(lambda q, _d=dbg_out[nm], _s=pairs[nm]: q.dma_start(out=_d[:, :], in_=_s))

    Q.ops.append(lambda q, _v=Q.n: q.wait_ge(_SEMH["q"], _v))

    with nc.Block() as block:
        @block.sync
        def _(sync):
            for fn in Q.ops:
                fn(sync)

        @block.tensor
        def _(tensor):
            for fn in TE.ops:
                fn(tensor)

        @block.vector
        def _(vector):
            for fn in V.ops:
                fn(vector)

        @block.scalar
        def _(scalar):
            for fn in SE.ops:
                fn(scalar)

        @block.gpsimd
        def _(gpsimd):
            for fn in G.ops:
                fn(gpsimd)

    es.close()
    return nc, dbg_out


def kernel(L0_attn_x, L0_ffn_w_gate, L0_ffn_w_up, L0_ffn_w_down):
    x = np.ascontiguousarray(np.asarray(L0_attn_x, dtype=np.float32))
    wg = np.ascontiguousarray(np.asarray(L0_ffn_w_gate, dtype=np.float32))
    wu = np.ascontiguousarray(np.asarray(L0_ffn_w_up, dtype=np.float32))
    wd = np.ascontiguousarray(np.asarray(L0_ffn_w_down, dtype=np.float32))
    nc, _ = build_program()
    in_maps = [{"x": x[2 * c:2 * c + 2], "wu": wu, "wg": wg, "wd": wd}
               for c in range(NCORES)]
    res = run_bass_kernel_spmd(nc, in_maps, core_ids=list(range(NCORES)))
    return np.concatenate([r["out"] for r in res.results], axis=0)
